# version 37
# baseline (speedup 1.0000x reference)
import numpy as np
import concourse.bass as bass
import concourse.mybir as mybir
from concourse.bass_utils import run_bass_kernel_spmd

F32 = mybir.dt.float32
BF16 = mybir.dt.bfloat16
AF = mybir.ActivationFunctionType
ALU = mybir.AluOpType
AX = mybir.AxisListType

T = 2048
D = 1024
HID = 4096
NH = 16
DK = 64
M = 256
NTAU = 16
TC = 512
NTC = T // TC
HC = HID // 128
DC = D // 128
LN16 = float(np.log(16.0))
EPS_PHI = 1e-6

DO_SUB1 = True
DO_ATTN = True
DO_SUB3 = True

ENGINES = ("pe", "act", "dve", "pool", "sp")


class Emitter:

    NSP = 12

    def __init__(self, nc):
        self.nc = nc
        self.ops = []
        self.last_writer = {}
        self.readers = {}
        self.fence_set = set()
        self.sems = {e: nc.alloc_semaphore(f"esem_{e}")
                     for e in ("pe", "act", "dve", "pool")}
        for k in range(self.NSP):
            self.sems[f"sp{k}"] = nc.alloc_semaphore(f"esem_sp{k}")
        self.n_sp = 0

    def op(self, eng, fn, rd=(), wr=(), tag=""):
        o = {"eng": eng, "fn": fn, "deps": set(), "inc": False,
             "tag": tag, "idx": len(self.ops)}
        if eng == "sp":
            k = self.n_sp % self.NSP
            o["sem_eng"] = f"sp{k}"
            o["inc"] = True
            o["ring_prev"] = (self.n_sp - self.NSP
                              if self.n_sp >= self.NSP else None)
            self.n_sp += 1
        else:
            o["sem_eng"] = eng
        deps = o["deps"]

        def add_dep(p):
            if p is None or p is o:
                return
            if p["eng"] == "pe" and eng == "pe":
                return
            deps.add(p["idx"])

        for r in rd:
            add_dep(self.last_writer.get(r))
        for w in wr:
            add_dep(self.last_writer.get(w))
            for p in self.readers.get(w, {}).values():
                add_dep(p)
        for d in self.fence_set:
            if d < o["idx"]:
                p = self.ops[d]
                if not (p["eng"] == "pe" and eng == "pe"):
                    deps.add(d)
        for r in rd:
            self.readers.setdefault(r, {})[eng] = o
        for w in wr:
            self.last_writer[w] = o
            self.readers[w] = {}
        self.ops.append(o)
        return o

    def fence(self):
        s = set()
        for p in self.last_writer.values():
            s.add(p["idx"])
        for lst in self.readers.values():
            for p in lst.values():
                s.add(p["idx"])
        self.fence_set = s

    def emit(self):
        nc = self.nc
        ops = self.ops
        sp_ops = [o for o in ops if o["eng"] == "sp"]
        for o in ops:
            for d in o["deps"]:
                ops[d]["inc"] = True
        sem_names = list(self.sems.keys())
        cnt = {e: 0 for e in sem_names}
        for o in ops:
            se = o["sem_eng"]
            if o["inc"]:
                cnt[se] += 16 if o["eng"] == "sp" else 1
            o["inc_val"] = cnt[se]
        clocks = {e: {f: 0 for f in sem_names} for e in
                  ("pe", "act", "dve", "pool", "sp")}
        snaps = {}
        for o in ops:
            e = o["eng"]
            ce = clocks[e]
            need = {}
            if o.get("ring_prev") is not None:
                p = sp_ops[o["ring_prev"]]
                need[p["sem_eng"]] = p["inc_val"]
            for d in o["deps"]:
                p = ops[d]
                v = p["inc_val"]
                if v > need.get(p["sem_eng"], 0):
                    need[p["sem_eng"]] = v
            waits = []
            for f, v in sorted(need.items()):
                if ce[f] < v:
                    waits.append((f, v))
                    ce[f] = v
                    for g, gv in snaps[(f, v)].items():
                        if ce[g] < gv:
                            ce[g] = gv
            o["waits"] = waits
            if o["inc"]:
                snap = dict(ce)
                snap[o["sem_eng"]] = o["inc_val"]
                snaps[(o["sem_eng"], o["inc_val"])] = snap
        streams = {e: [] for e in ("pe", "act", "dve", "pool", "sp")}
        for o in ops:
            streams[o["eng"]].append(o)
        sems = self.sems

        def run_stream(eng_obj, eng_name):
            step = 16 if eng_name == "sp" else 1
            for o in streams[eng_name]:
                for f, v in o["waits"]:
                    eng_obj.wait_ge(sems[f], v)
                inst = o["fn"](eng_obj)
                if o["inc"]:
                    inst.then_inc(sems[o["sem_eng"]], step)

        with nc.Block() as block:
            @block.tensor
            def _(te):
                run_stream(te, "pe")

            @block.scalar
            def _(sc):
                run_stream(sc, "act")

            @block.vector
            def _(ve):
                run_stream(ve, "dve")

            @block.gpsimd
            def _(gp):
                run_stream(gp, "pool")

            @block.sync
            def _(sy):
                run_stream(sy, "sp")


def build_core_program():
    nc = bass.Bass("TRN2", target_bir_lowering=False)

    x_d = nc.dram_tensor("x", [T, D], F32, kind="ExternalInput").ap()
    w1s_d = nc.dram_tensor("w1s", [HC, D, 128], F32, kind="ExternalInput").ap()
    w2t_d = nc.dram_tensor("w2t", [HID, D], F32, kind="ExternalInput").ap()
    wqs_d = nc.dram_tensor("wqs", [DC, D, 128], F32, kind="ExternalInput").ap()
    wks_d = nc.dram_tensor("wks", [DC, D, 128], F32, kind="ExternalInput").ap()
    wvt_d = nc.dram_tensor("wvt", [D, D], F32, kind="ExternalInput").ap()
    wot_d = nc.dram_tensor("wot", [D, D], F32, kind="ExternalInput").ap()
    proj_d = nc.dram_tensor("projp", [NH, 128, M], F32, kind="ExternalInput").ap()
    b1t_d = nc.dram_tensor("b1t", [128, HC], F32, kind="ExternalInput").ap()
    b2_d = nc.dram_tensor("b2", [D], F32, kind="ExternalInput").ap()
    bo_d = nc.dram_tensor("bo", [D], F32, kind="ExternalInput").ap()
    idb_d = nc.dram_tensor("idb", [128, 128], BF16, kind="ExternalInput").ap()
    idf_d = nc.dram_tensor("idf", [2, 2], F32, kind="ExternalInput").ap()
    onesb_d = nc.dram_tensor("onesb", [128], BF16, kind="ExternalInput").ap()
    onesf_d = nc.dram_tensor("onesf", [128], F32, kind="ExternalInput").ap()
    blk2_d = nc.dram_tensor("blk2", [128, 2], BF16, kind="ExternalInput").ap()
    out_d = nc.dram_tensor("out", [T, D], F32, kind="ExternalOutput").ap()
    sgn1c_d = nc.dram_tensor("sgn1c", [HC, 128, D], BF16, kind="Internal").ap()
    bw2c_d = nc.dram_tensor("bw2c", [128, HC * D], BF16, kind="Internal").ap()
    sv_d = nc.dram_tensor("sv_dram", [1, 1040], F32, kind="Internal").ap()

    A = nc.alloc_sbuf_tensor
    x_sb = A("x_sb", [128, NTAU * D], F32)
    uT = A("uT", [128, 16384], BF16)
    ar1 = A("ar1", [128, 32768], BF16)
    ar2 = A("ar2", [128, 16384], BF16)
    wbuf = [A(f"wbuf{i}", [128, 1024], F32) for i in range(2)]
    wbf = [A(f"wbf{i}", [128, 1024], BF16) for i in range(2)]
    idb = A("idb_s", [128, 128], BF16)
    idf = A("idf_s", [2, 2], F32)
    onesb = A("onesb_s", [1, 128], BF16)
    onescol = A("onescol", [128, 1], BF16)
    onesf = A("onesf_s", [1, 128], F32)
    blk2 = A("blk2_s", [128, 2], BF16)
    b1sb = A("b1sb", [128, HC], F32)
    alpha1 = A("alpha1", [128, HC], F32)
    epsln = A("epsln", [128, 1], F32)
    lnt = A("lnt", [128, 1], F32)
    stats = [A(f"stats{i}", [128, 12], F32) for i in range(2)]
    mv = [A(f"mv{i}", [128, 2], F32) for i in range(2)]
    sd = [A(f"sd{i}", [128, 1], F32) for i in range(2)]
    rstd = [A(f"rstd{i}", [128, 1], F32) for i in range(2)]
    qnq = A("qnq", [128, 2 * NTAU], F32)
    qnk = A("qnk", [128, 2 * NTAU], F32)
    negmx4 = A("negmx4", [128, 16], F32)
    den4 = [A(f"den4_{i}", [128, 4], F32) for i in range(2)]
    rec4 = [A(f"rec4_{i}", [128, 4], F32) for i in range(2)]
    kv_sbA = A("kv_sbA", [128, 130], BF16)
    sve_pr = A("sve_pr", [1, 130], F32)
    csum_sb = A("csum_sb", [1, 130], F32)
    U1OFF = [0, 9472]
    U1OFF_B = 9472
    wabs = uT.ap()[:, 4096:5120]
    browF = uT.ap()[:, 5120:7168].bitcast(F32)[0:1, 0:1024]
    a1row = uT.ap()[:, 7168:7424].bitcast(F32)[0:1, 0:128]
    u_tmF = [uT.ap()[:, 7424 + i * 1024:7424 + (i + 1) * 1024]
             for i in range(2)]
    b2bc = uT.ap()[:, 13568:15616].bitcast(F32)
    w2buf = [uT.ap()[:, U1OFF_B + i * 2048:U1OFF_B + (i + 1) * 2048].bitcast(F32)
             for i in range(2)]
    a2bc = w2buf[0]
    u_tmA = [ar1.ap()[:, 24832 + i * 1024:24832 + (i + 1) * 1024]
             for i in range(2)]
    borow = None
    projb = [ar1.ap()[:, 30976 + i * M:30976 + (i + 1) * M] for i in range(2)]
    kvt_sb = [ar1.ap()[0:65, 31488 + i * M:31488 + (i + 1) * M]
              for i in range(2)]
    kv_sb2 = ar1.ap()[:, 32000:32130]
    kv_sb = [kv_sbA.ap(), kv_sb2]
    att_st = [ar1.ap()[:, 32130 + i * 128:32130 + (i + 1) * 128]
              for i in range(2)]

    def bw2_v(hc):
        return ar1.ap()[:, hc * D:(hc + 1) * D]
    V_COLS = NH * 65

    def v_v(tau, h):
        base = tau * V_COLS + h * 65
        return ar1.ap()[:, base:base + 65]
    WV_OFF = NTAU * V_COLS

    def wv_v(dc):
        return ar1.ap()[:, WV_OFF + dc * D: WV_OFF + (dc + 1) * D]
    QK_OFF = WV_OFF + DC * D
    borow = ar1.ap()[:, QK_OFF:QK_OFF + 2048].bitcast(F32)[0:1, 0:1024]
    q_ec = ar1.ap()[:, QK_OFF:QK_OFF + T]
    k_ec = ar1.ap()[:, QK_OFF + T:QK_OFF + 2 * T]
    qsq = ar1.ap()[:, QK_OFF + 2 * T:QK_OFF + 3 * T]

    def g_v(hc, t0, t1):
        return ar2.ap()[:, hc * TC + t0: hc * TC + t1]

    def att_v(tau, lo, hi):
        return ar2.ap()[:, tau * D + lo: tau * D + hi]

    pb = [nc.alloc_psum_tensor(f"pb{i}", [128, 512], F32) for i in range(8)]

    def pbf(i):
        return pb[i].ap()[:, 0:64].bitcast(BF16)

    E = Emitter(nc)

    def mm(out, lhsT, rhs, start, stop, rd, wr, tag=""):
        E.op("pe", lambda e, o=out, l=lhsT, r=rhs, s1=start, s2=stop:
             e.matmul(o, l, r, start=s1, stop=s2, skip_group_check=True),
             rd=rd, wr=wr, tag=tag)

    def tr(out, in_, ident, rd, wr, tag=""):
        E.op("pe", lambda e, o=out, i=in_, d=ident: e.transpose(o, i, d),
             rd=rd, wr=wr, tag=tag)

    def cp(eng, out, in_, rd, wr, tag=""):
        if eng == "act":
            E.op("act", lambda e, o=out, i=in_: e.copy(o, i),
                 rd=rd, wr=wr, tag=tag)
        else:
            E.op("dve", lambda e, o=out, i=in_: e.tensor_copy(o, i),
                 rd=rd, wr=wr, tag=tag)

    def dma(dst, src, rd, wr, tag=""):
        E.op("sp", lambda e, d=dst, s=src: e.dma_start(d, s),
             rd=rd, wr=wr, tag=tag)

    x_src = x_d.rearrange("(t p) d -> p t d", p=128)
    x_dst = x_sb.ap().rearrange("p (t d) -> p t d", d=D)
    for tc4 in range(4):
        dma(x_dst[:, tc4 * 4:(tc4 + 1) * 4, :],
            x_src[:, tc4 * 4:(tc4 + 1) * 4, :],
            [], [f"x.{t}" for t in range(tc4 * 4, tc4 * 4 + 4)], tag="ldx")
    for dst, src, key in [
        (idb.ap(), idb_d, "idb"), (idf.ap(), idf_d, "idf"),
        (onesb.ap()[0:1, :], onesb_d.unsqueeze(0), "onesb"),
        (onescol.ap()[:, 0:1], onesb_d.unsqueeze(1), "onescol"),
        (onesf.ap()[0:1, :], onesf_d.unsqueeze(0), "onesf"),
        (blk2.ap(), blk2_d, "blk2"), (b1sb.ap(), b1t_d, "b1sb"),
    ]:
        dma(dst, src, [], [key])
    E.op("dve", lambda e: e.memset(epsln.ap(), 1e-5), wr=["epsln"])
    E.op("dve", lambda e: e.memset(lnt.ap(), -LN16), wr=["lnt"])

    def ln_transpose(tau, col_of, key_of, utm, ukey, fill=None,
                     extra_wr=()):
        r = tau % 2
        xk = f"x.{tau}"
        xv = x_sb.ap()[:, tau * D:(tau + 1) * D]
        out = []
        out.append(lambda: E.op(
            "dve", lambda e, s=stats[r].ap()[:, 0:6], x=xv[:, 0:512]:
            e.bn_stats(s, x), rd=[xk], wr=[f"st{r}a"]))
        out.append(lambda: E.op(
            "dve", lambda e, s=stats[r].ap()[:, 6:12], x=xv[:, 512:1024]:
            e.bn_stats(s, x), rd=[xk], wr=[f"st{r}b"]))
        out.append(lambda: E.op(
            "dve", lambda e, o=mv[r].ap(), i=stats[r].ap(): e.bn_aggr(o, i),
            rd=[f"st{r}a", f"st{r}b"], wr=[f"mv{r}"]))
        out.append(lambda: E.op(
            "act", lambda e, o=sd[r].ap(), i=mv[r].ap()[:, 1:2]:
            e.activation(o, i, AF.Sqrt, bias=epsln.ap()),
            rd=[f"mv{r}", "epsln"], wr=[f"sd{r}"]))
        out.append(lambda: E.op(
            "dve", lambda e, o=rstd[r].ap(), i=sd[r].ap(): e.reciprocal(o, i),
            rd=[f"sd{r}"], wr=[f"rstd{r}"]))
        out.append(lambda: E.op(
            "dve", lambda e, o=utm[r], x=xv, m=mv[r].ap()[:, 0:1],
            s=rstd[r].ap():
            e.tensor_scalar(o, x, m, s, ALU.subtract, ALU.mult),
            rd=[xk, f"mv{r}", f"rstd{r}"],
            wr=[f"{ukey}{r}"] + list(extra_wr)))
        for dc in range(DC):
            bk = 6 + (dc % 2)

            def unit(dc=dc, bk=bk):
                tr(pbf(bk), utm[r][:, dc * 128:(dc + 1) * 128], idb.ap(),
                   rd=[f"{ukey}{r}", "idb"], wr=[f"pb{bk}"])
                col = col_of(dc)
                eng = "dve" if dc % 2 == 0 else "act"
                cp(eng, uT.ap()[:, col:col + 128], pbf(bk),
                   rd=[f"pb{bk}"], wr=[key_of(dc)] + list(extra_wr))
            out.append(unit)
        if fill is None:
            for f in out:
                f()
        else:
            fill.extend(out)

    def ffn(first, wo=None):
        bwkeys = [f"bw2.{hc}" for hc in range(HC)]
        bw2_part_b = None
        if first:
            def w2_load(hc):
                r = hc % 2
                for hf in range(2):
                    dma(w2buf[r][:, hf * 512:(hf + 1) * 512],
                        w2t_d[hc * 128:(hc + 1) * 128,
                              hf * 512:(hf + 1) * 512],
                        [], ["w2buf%d%s" % (r, "ab"[hf])])

            def w2_unit(hc):
                r = hc % 2
                k2 = ["w2buf%da" % r, "w2buf%db" % r]
                E.op("dve", lambda e, i=w2buf[r]: e.scalar_tensor_tensor(
                    wabs, i, -1.0, i, ALU.mult, ALU.max),
                    rd=k2, wr=["wabs"])
                E.op("act", lambda e, o=bw2_v(hc), i=w2buf[r]:
                     e.sign(o, i), rd=k2, wr=[f"bw2.{hc}"])
                if hc + 2 < HC:
                    w2_load(hc + 2)
                for half in range(2):
                    mm(pb[6 + half].ap()[0:1, 0:512], onescol.ap(),
                       wabs[:, half * 512:(half + 1) * 512],
                       hc == 0, hc == HC - 1,
                       rd=["wabs", "onescol"], wr=[f"pb{6 + half}"])
            for hc in range(2):
                w2_load(hc)
            w2fill = list(range(HC))
            def w2_finish():
                for half in range(2):
                    E.op("dve", lambda e,
                         o=browF[0:1, half * 512:(half + 1) * 512],
                         i=pb[6 + half].ap()[0:1, 0:512]:
                         e.tensor_scalar(o, i, 0.5 / HID, None, ALU.mult),
                         rd=[f"pb{6 + half}"], wr=["browF"])
                for half in range(2):
                    mm(pb[half].ap(), onesf.ap(),
                       browF[0:1, half * 512:(half + 1) * 512], True, True,
                       rd=["browF", "onesf"], wr=[f"pb{half}"])
                    cp("dve", a2bc[:, half * 512:(half + 1) * 512],
                       pb[half].ap(), rd=[f"pb{half}"], wr=["w2buf0"])
                for hc in range(HC):
                    E.op("dve", lambda e, o=bw2_v(hc): e.tensor_tensor(
                        o, o, a2bc, ALU.mult),
                        rd=[f"bw2.{hc}", "w2buf0"], wr=[f"bw2.{hc}"])
                dma(bw2c_d, ar1.ap()[:, 0:HC * D], bwkeys, ["bw2c"])
        else:
            vkeys = [f"v.{t}.{hf}" for t in range(NTAU) for hf in range(2)]
            restkeys = ([f"wv.{dc}" for dc in range(DC)]
                        + ["qec0", "qec1", "kec0", "kec1",
                           "u_tmA0", "u_tmA1"]
                        + [f"phisl{i}" for i in range(16)]
                        + [f"qsqseg{i}" for i in range(8)]
                        + ["kvt0", "kvt1", "kv0", "kv1", "qnq", "qnk",
                           "qn_st", "bobc", "vones", "borow",
                           "projb0", "projb1", "attst0", "attst1"])
            if wo is None:
                dma(ar1.ap()[:, 0:HC * D], bw2c_d, ["bw2c"],
                    bwkeys + vkeys + restkeys)
            else:
                dma(ar1.ap()[:, 0:16 * D], bw2c_d[:, 0:16 * D], ["bw2c"],
                    bwkeys[:16] + vkeys + ["vones"])

                def bw2_part_b():
                    dma(ar1.ap()[:, 16 * D:HC * D], bw2c_d[:, 16 * D:HC * D],
                        ["bw2c"], bwkeys[16:] + vkeys[30:] + restkeys)
        def b2_setup():
            dma(browF, b2_d.unsqueeze(0), ["browF"],
                ["browF"] if first else ["browF", "uTall"])
            E.op("dve", lambda e: e.tensor_scalar(browF, browF, 0.5, None,
                                                  ALU.mult),
                 rd=["browF"], wr=["browF"])
            for half in range(2):
                mm(pb[6 + half].ap(), onesf.ap(),
                   browF[0:1, half * 512:(half + 1) * 512], True, True,
                   rd=["browF", "onesf"], wr=[f"pb{6 + half}"])
                cp("dve", b2bc[:, half * 512:(half + 1) * 512],
                   pb[6 + half].ap(), rd=[f"pb{6 + half}"],
                   wr=["b2bc"] if first else ["b2bc", "uTall"])
        if not first:
            b2_setup()

        def emit_ln(tc, fill=None):
            pbuf = tc % 2
            extra = ["w2buf0", "w2buf1"] if (first and tc == 1) else []
            if not first:
                extra = extra + ["uTall"]
            for tau in range(tc * 4, tc * 4 + 4):
                j = tau - tc * 4
                ln_transpose(tau,
                             lambda dc, j=j, p=pbuf: U1OFF[p] + dc * TC + j * 128,
                             lambda dc, j=j, p=pbuf: f"u1T.{p}.{dc}.{j}",
                             u_tmF, "u_tmF", fill=fill, extra_wr=extra)

        wo_next = 0

        def wo_ensure(upto):
            nonlocal wo_next
            while wo_next <= min(upto, NTAU - 1):
                wo[1](wo_next)
                wo_next += 1

        if wo is not None:
            wo[0]()
            wo_ensure(3)
        emit_ln(0)
        for tc in range(NTC):
            pbuf = tc % 2
            u1o = U1OFF[pbuf]
            stream_fp32 = first and tc == 0

            def w1_load(hc):
                r = hc % 2
                if stream_fp32:
                    for hf in range(2):
                        dma(wbuf[r].ap()[:, hf * 512:(hf + 1) * 512].rearrange(
                            "p (dc h) -> p dc h", h=128),
                            w1s_d[hc, hf * 512:(hf + 1) * 512, :].rearrange(
                                "(dc p) h -> p dc h", p=128),
                            [], ["wbuf%d%s" % (r, "ab"[hf])])
                else:
                    for hf in range(2):
                        dma(wbf[r].ap()[:, hf * 512:(hf + 1) * 512],
                            sgn1c_d[hc, :, hf * 512:(hf + 1) * 512],
                            ["sgn1c"], ["wbf%d%s" % (r, "ab"[hf])])
            for hc in range(2):
                w1_load(hc)
            for hc in range(HC):
                r = hc % 2
                if wo is not None and tc == 0:
                    wo_ensure(hc // 2 + 4)
                if stream_fp32:
                    E.op("dve", lambda e, i=wbuf[r].ap(): e.scalar_tensor_tensor(
                        wabs, i, -1.0, i, ALU.mult, ALU.max),
                        rd=["wbuf%da" % r, "wbuf%db" % r], wr=["wabs"])
                    E.op("act", lambda e, o=wbf[r].ap(), i=wbuf[r].ap():
                         e.sign(o, i), rd=["wbuf%da" % r, "wbuf%db" % r],
                         wr=["wbf%da" % r, "wbf%db" % r])
                    for dc in range(DC):
                        mm(pb[5].ap()[0:1, 0:128], onescol.ap(),
                           wabs[:, dc * 128:(dc + 1) * 128],
                           dc == 0, dc == DC - 1,
                           rd=["wabs", "onescol"], wr=["pb5"])
                    E.op("dve", lambda e, o=a1row[0:1, :],
                         i=pb[5].ap()[0:1, 0:128]:
                         e.tensor_scalar(o, i, 1.0 / D, None, ALU.mult),
                         rd=["pb5"], wr=["a1row"])
                    tr(pb[4].ap()[:, 0:1], a1row[0:1, :],
                       idf.ap()[0:1, 0:1], rd=["a1row", "idf"], wr=["pb4"])
                    cp("dve", alpha1.ap()[:, hc:hc + 1], pb[4].ap()[:, 0:1],
                       rd=["pb4"], wr=[f"alpha1.{hc}"])
                    dma(sgn1c_d[hc], wbf[r].ap(), ["wbf%da" % r, "wbf%db" % r], ["sgn1c"])
                hb = hc % 2
                for dc in range(DC):
                    mm(pb[hb].ap(), wbf[r].ap()[:, dc * 128:(dc + 1) * 128],
                       uT.ap()[:, u1o + dc * TC:u1o + (dc + 1) * TC],
                       dc == 0, dc == DC - 1,
                       rd=["wbf%d%s" % (r, "ab"[dc // 4])] +
                          [f"u1T.{pbuf}.{dc}.{j}" for j in range(4)],
                       wr=[f"pb{hb}"])
                if hc + 2 < HC:
                    w1_load(hc + 2)
                E.op("act", lambda e, o=g_v(hc, 0, TC), i=pb[hb].ap(),
                     b=b1sb.ap()[:, hc:hc + 1], s=alpha1.ap()[:, hc:hc + 1]:
                     e.activation(o, i, AF.Gelu, bias=b, scale=s),
                     rd=[f"pb{hb}", f"alpha1.{hc}", "b1sb"],
                     wr=[f"g.{hc}"] if first else
                        [f"g.{hc}", f"att.{hc // 2}"])
                if stream_fp32:
                    w2_unit(hc)
                    if hc == HC - 1:
                        w2_finish()
                        b2_setup()
            if wo is not None and tc == 0:
                wo_ensure(NTAU - 1)
                bw2_part_b()
            lnfill = []
            if tc + 1 < NTC:
                emit_ln(tc + 1, fill=lnfill)
            li = 0
            for tp in range(2):
                banks = [2, 3, 4, 5]
                for hc in range(HC):
                    for jj in range(4):
                        ti = tp * 2 + jj // 2
                        half = jj % 2
                        bk = banks[jj]
                        mm(pb[bk].ap(), g_v(hc, ti * 128, ti * 128 + 128),
                           bw2_v(hc)[:, half * 512:(half + 1) * 512],
                           hc == 0, False,
                           rd=[f"g.{hc}", f"bw2.{hc}"], wr=[f"pb{bk}"])
                    if li < len(lnfill):
                        lnfill[li]()
                        li += 1
                    if tp == 1 and hc >= HC - 4:
                        while li < len(lnfill):
                            lnfill[li]()
                            li += 1
                for jj in range(4):
                    ti = tp * 2 + jj // 2
                    half = jj % 2
                    bk = banks[jj]
                    E.op("dve", lambda e, p=pb[bk].ap(),
                         bb=b2bc[:, half * 512:(half + 1) * 512]:
                         e.tensor_tensor(p, p, bb, ALU.add),
                         rd=[f"pb{bk}", "b2bc"], wr=[f"pb{bk}"])
                    tau = tc * 4 + ti
                    xv = x_sb.ap()[:, tau * D + half * 512:
                                   tau * D + half * 512 + 512]
                    E.op("dve", lambda e, xv=xv, p=pb[bk].ap():
                         e.scalar_tensor_tensor(xv, p, 1.0, xv,
                                                ALU.mult, ALU.add),
                         rd=[f"pb{bk}", f"x.{tau}"], wr=[f"x.{tau}"])

    def attention():
        def v_tau3(tau):
            base = tau * V_COLS
            return ar1.ap()[:, base:base + V_COLS].rearrange(
                "p (h c) -> p h c", c=65)
        lnf = []
        for tau in range(NTAU):
            ln_transpose(tau, lambda dc, tau=tau: dc * T + tau * 128,
                         lambda dc, tau=tau: f"u2T.{dc}.{tau}", u_tmA, "u_tmA",
                         fill=lnf)
        def wv_load_wvt(dc):
            for hf in range(2):
                dma(wbuf[dc % 2].ap()[:, hf * 512:(hf + 1) * 512],
                    wvt_d[dc * 128:(dc + 1) * 128,
                            hf * 512:(hf + 1) * 512],
                    [], ["wbuf%d%s" % (dc % 2, "ab"[hf])])
        for dc in range(2):
            wv_load_wvt(dc)
        for dc in range(DC):
            r = dc % 2
            cp("act", wv_v(dc), wbuf[r].ap(),
               rd=["wbuf%da" % r, "wbuf%db" % r], wr=[f"wv.{dc}"])
            if dc + 2 < DC:
                wv_load_wvt(dc + 2)
        NU = len(lnf) // NTAU
        for f in lnf[0:2 * NU]:
            f()
        for tau0 in range(0, NTAU, 2):
            for f in lnf[(tau0 + 2) * NU:(tau0 + 4) * NU]:
                f()
            for tau in (tau0, tau0 + 1):
                for half in range(2):
                    bk = (tau * 2 + half) % 2
                    for dc in range(DC):
                        mm(pb[bk].ap(),
                           uT.ap()[:, dc * T + tau * 128:
                                   dc * T + tau * 128 + 128],
                           wv_v(dc)[:, half * 512:(half + 1) * 512],
                           dc == 0, dc == DC - 1,
                           rd=[f"wv.{dc}", f"u2T.{dc}.{tau}", "uTall"],
                           wr=[f"pb{bk}"])
                    dst = v_tau3(tau)[:, half * 8:half * 8 + 8, 0:64]
                    srcv = pb[bk].ap().rearrange("p (h c) -> p h c", c=64)
                    E.op("act", lambda e, d=dst, s=srcv: e.copy(d, s),
                         rd=[f"pb{bk}"], wr=[f"v.{tau}.{half}"])
                E.op("dve", lambda e, v=v_tau3(tau)[:, :, 64:65]:
                     e.memset(v, 1.0), wr=["vones"])

        for g4 in range(4):
            bkv = [2, 3, 6, 7][g4]
            for t_ in range(NTAU):
                vb = t_ * V_COLS + g4 * 260
                mm(pb[bkv].ap()[0:1, 0:260], onescol.ap(),
                   ar1.ap()[:, vb:vb + 260], t_ == 0, t_ == NTAU - 1,
                   rd=[f"v.{t_}.{g4 // 2}", "vones", "onescol"],
                   wr=[f"pb{bkv}"], tag="sv")
        svbufs = [(sve_pr, "svepr0"), (csum_sb, "svepr1")]
        for pr8 in range(8):
            bkv = [2, 3, 6, 7][pr8 // 2]
            stg, skey = svbufs[pr8 % 2]
            E.op("act", lambda e, o=stg.ap()[0:1, 0:130],
                 i=pb[bkv].ap()[0:1, (pr8 % 2) * 130:(pr8 % 2) * 130 + 130]:
                 e.activation(o, i, AF.Copy, bias=0.0, scale=EPS_PHI / 16.0),
                 rd=[f"pb{bkv}"], wr=[skey])
            dma(sv_d[0:1, pr8 * 130:(pr8 + 1) * 130], stg.ap()[0:1, 0:130],
                [skey], [f"svdram{pr8}"])

        qk_buf = [(q_ec, k_ec, "qec0", "kec0"),
                  (ar1.ap()[:, WV_OFF:WV_OFF + T],
                   ar1.ap()[:, WV_OFF + T:WV_OFF + 2 * T], "qec1", "kec1")]

        def emit_qkv(p, fill=None):
            qv, kv, qkey, kkey = qk_buf[p % 2]
            wv_war = ((["wv.0", "wv.1"], ["wv.2", "wv.3"]) if p % 2 == 1
                      else (["u_tmA0", "u_tmA1"], []))
            out = []
            for w_i, wsl, dst, dkey, war in [(0, wqs_d, qv, qkey, wv_war[0]),
                                             (1, wks_d, kv, kkey, wv_war[1])]:
                def qk_load(w_i=w_i, wsl=wsl):
                    for hf in range(2):
                        dma(wbuf[w_i].ap()[:, hf * 512:(hf + 1) * 512]
                            .rearrange("p (dc h) -> p dc h", h=128),
                            wsl[p, hf * 512:(hf + 1) * 512, :].rearrange(
                                "(dc p) h -> p dc h", p=128),
                            [], ["wbuf%d%s" % (w_i, "ab"[hf])])
                out.append(qk_load)
                out.append(lambda w_i=w_i: E.op(
                    "dve", lambda e, o=wbf[w_i].ap(), i=wbuf[w_i].ap():
                    e.tensor_copy(o, i),
                    rd=["wbuf%da" % w_i, "wbuf%db" % w_i],
                    wr=[f"wbf{w_i}"]))
                for tcq in range(4):
                    bk = 6 + tcq % 2 if fill is not None else tcq % 2
                    def grp(w_i=w_i, tcq=tcq, bk=bk, dst=dst, dkey=dkey,
                            war=war):
                        for dc in range(DC):
                            mm(pb[bk].ap(),
                               wbf[w_i].ap()[:, dc * 128:(dc + 1) * 128],
                               uT.ap()[:, dc * T + tcq * 512:
                                       dc * T + tcq * 512 + 512],
                               dc == 0, dc == DC - 1,
                               rd=[f"wbf{w_i}", "uTall"] +
                                  [f"u2T.{dc}.{tcq * 4 + j}"
                                   for j in range(4)],
                               wr=[f"pb{bk}"])
                        E.op("act", lambda e,
                             o=dst[:, tcq * 512:(tcq + 1) * 512],
                             i=pb[bk].ap():
                             e.activation(o, i, AF.Copy, bias=0.0,
                                          scale=0.125),
                             rd=[f"pb{bk}"], wr=[dkey] + war)
                    out.append(grp)
            if fill is None:
                for f in out:
                    f()
            else:
                fill.extend(out)

        emit_qkv(0)
        segk = [f"qsqseg{i}" for i in range(8)]
        for pr in range(8):
            qv, kv, qkey, kkey = qk_buf[pr % 2]
            PHI_OFF = WV_OFF + 4096
            qnst = [ar1.ap()[:, PHI_OFF + (8 + 2 * i) * M:
                             PHI_OFF + (10 + 2 * i) * M].bitcast(F32)[0:2, :]
                    for i in range(2)]
            qnst_k = [["phisl8", "phisl9"], ["phisl10", "phisl11"]]
            for src_, skey, qdst, qk_ in [(qv, qkey, qnq, "qnq"),
                                          (kv, kkey, qnk, "qnk")]:
                E.op("act", lambda e, i=src_: e.activation(qsq, i, AF.Square),
                     rd=[skey], wr=segk)
                for g2 in range(10):
                    if g2 < 8:
                        qb = 4 + (g2 % 2)
                        mm(pb[qb].ap()[0:2, 0:256], blk2.ap(),
                           qsq[:, g2 * 256:(g2 + 1) * 256], True, True,
                           rd=[segk[g2], "blk2"], wr=[f"pb{qb}"])
                        E.op("dve", lambda e, o=qnst[g2 % 2],
                             i=pb[qb].ap()[0:2, 0:256]:
                             e.tensor_copy(o, i),
                             rd=[f"pb{qb}"], wr=qnst_k[g2 % 2])
                    if g2 >= 2:
                        g = g2 - 2
                        dst2 = qdst.ap().rearrange(
                            "p (h t) -> p h t", t=NTAU)[:, :, 2 * g:2 * g + 2]
                        for j in range(2):
                            cp("dve",
                               dst2[:, :, j:j + 1],
                               pb[6 + j].ap()[:, j * 2:j * 2 + 2].unsqueeze(2),
                               rd=[f"pb{6 + j}"], wr=[qk_])
                    if 1 <= g2 <= 8:
                        g = g2 - 1
                        for j in range(2):
                            tr(pb[6 + j].ap()[:, j * 2:j * 2 + 2],
                               qnst[g % 2][0:2, j * 128:(j + 1) * 128],
                               idf.ap(),
                               rd=qnst_k[g % 2] + ["idf"], wr=[f"pb{6 + j}"])
                E.op("act", lambda e, o=qdst.ap(): e.activation(
                    o, o, AF.Copy, bias=-LN16, scale=-0.5),
                    rd=[qk_], wr=[qk_])
            svb, svkey = svbufs[pr % 2]
            dma(svb.ap()[0:1, 0:130], sv_d[0:1, pr * 130:(pr + 1) * 130],
                [f"svdram{pr}"], [svkey])
            for hh in range(2):
                h = pr * 2 + hh
                dma(wbuf[hh].ap()[:, 0:M], proj_d[h], [],
                    ["wbuf%da" % hh])
                cp("act", projb[hh], wbuf[hh].ap()[:, 0:M],
                   rd=["wbuf%da" % hh], wr=[f"projb{hh}"])

            def phis_i(i):
                return (ar1.ap()[:, PHI_OFF + i * M:PHI_OFF + (i + 1) * M],
                        f"phisl{i}")

            def phis(b, hh, r):
                return phis_i((b % 4) * 4 + hh * 2 + r)

            def pfms(b, hh, r):
                return phis(b, hh, r)

            def emit_xw_batch(b, hh, src_ec, skey):
                bk = hh * 4 + b % 2
                for j in range(2):
                    tau = b * 2 + j
                    mm(pb[bk].ap()[:, j * 256:(j + 1) * 256],
                       src_ec[:, tau * 128:(tau + 1) * 128],
                       projb[hh], True, True,
                       rd=[skey, f"projb{hh}"], wr=[f"pb{bk}"])

            def emit_phi_batch(b, qn_t, qn_key):
                for hh in range(2):
                    bk = hh * 4 + b % 2
                    for j in range(2):
                        sl = (b % 4) * 4 + hh * 2 + j
                        xwv = pb[bk].ap()[:, j * 256:(j + 1) * 256]
                        E.op("dve", lambda e, o=negmx4.ap()[:, sl:sl + 1],
                             i=xwv:
                             e.reduce_max(o, i, AX.X, negate=True),
                             rd=[f"pb{bk}"], wr=[f"negmx{sl}"])
                for hh in range(2):
                    sl0 = (b % 4) * 4 + hh * 2
                    col = hh * NTAU + b * 2
                    E.op("dve", lambda e, o=negmx4.ap()[:, sl0:sl0 + 2],
                         i1=qn_t.ap()[:, col:col + 2]:
                         e.tensor_tensor(o, o, i1, ALU.add),
                         rd=[f"negmx{sl0}", f"negmx{sl0 + 1}", qn_key],
                         wr=[f"negmx{sl0}", f"negmx{sl0 + 1}"])
                for hh in range(2):
                    bk = hh * 4 + b % 2
                    for j in range(2):
                        pst, pk = phis(b, hh, j)
                        sl = (b % 4) * 4 + hh * 2 + j
                        xwv = pb[bk].ap()[:, j * 256:(j + 1) * 256]
                        E.op("act", lambda e, o=pst, i=xwv,
                             bb=negmx4.ap()[:, sl:sl + 1]:
                             e.activation(o, i, AF.Exp, bias=bb),
                             rd=[f"pb{bk}", f"negmx{sl}"], wr=[pk])

            fill = []
            if pr + 1 < 8:
                emit_qkv(pr + 1, fill=fill)
            fi = 0
            def do_fill(n):
                nonlocal fi
                for _ in range(n):
                    if fi < len(fill):
                        fill[fi]()
                        fi += 1
            for b in range(9):
                if b < 8:
                    for hh in range(2):
                        emit_xw_batch(b, hh, kv, kkey)
                if b >= 1:
                    c = b - 1
                    for hh in range(2):
                        h = pr * 2 + hh
                        for j in range(2):
                            t_ = c * 2 + j
                            pst, pk = phis(c, hh, j)
                            mm(pb[2 + hh].ap()[0:65, 0:256], v_v(t_, h), pst,
                               t_ == 0, False,
                               rd=[pk, f"v.{t_}.{h // 8}", "vones"],
                               wr=[f"pb{2 + hh}"], tag="kvt")
                do_fill(6 if b in (3, 7) else 0)
                if b < 8:
                    emit_phi_batch(b, qnk, "qnk")
            for hh in range(2):
                kvt_stg = kvt_sb[hh]
                for mh in range(2):
                    mm(pb[2 + hh].ap()[0:65, mh * 128:(mh + 1) * 128],
                       svb.ap()[0:1, hh * 65:(hh + 1) * 65],
                       onesf.ap()[0:1, 0:128], False, True,
                       rd=[svkey, "onesf"], wr=[f"pb{2 + hh}"], tag="kvt")
                cp("act", kvt_stg, pb[2 + hh].ap()[0:65, 0:256],
                   rd=[f"pb{2 + hh}"], wr=[f"kvt{hh}"])
                for mc in range(2):
                    bk6 = 6 + mc
                    tr(pbf(bk6)[:, 0:65],
                       kvt_stg[0:65, mc * 128:(mc + 1) * 128],
                       idb.ap()[0:65, 0:65],
                       rd=[f"kvt{hh}", "idb"], wr=[f"pb{bk6}"])
                    cp("dve", kv_sb[hh][:, mc * 65:(mc + 1) * 65],
                       pbf(bk6)[:, 0:65], rd=[f"pb{bk6}"], wr=[f"kv{hh}"])
            def q_tail(c):
                dn, rc = den4[c % 2], rec4[c % 2]
                for hh in range(2):
                    nbk = (2 + c % 2) if hh == 0 else (6 + c % 2)
                    for j in range(2):
                        sl = hh * 2 + j
                        E.op("dve", lambda e, o=dn.ap()[:, sl:sl + 1],
                             i=pb[nbk].ap()[:, j * 65 + 64:j * 65 + 65]:
                             e.tensor_scalar(o, i, EPS_PHI, None, ALU.add),
                             rd=[f"pb{nbk}"], wr=[f"den4{c % 2}"])
                E.op("dve", lambda e, o=rc.ap(), i=dn.ap():
                     e.reciprocal(o, i),
                     rd=[f"den4{c % 2}"], wr=[f"rec4{c % 2}"])
                for hh in range(2):
                    h = pr * 2 + hh
                    nbk = (2 + c % 2) if hh == 0 else (6 + c % 2)
                    for j in range(2):
                        t_ = c * 2 + j
                        sl = hh * 2 + j
                        E.op("dve", lambda e,
                             o=att_v(t_, h * 64, h * 64 + 64),
                             i=pb[nbk].ap()[:, j * 65:j * 65 + 64],
                             s=rc.ap()[:, sl:sl + 1]:
                             e.tensor_scalar(o, i, s, None, ALU.mult),
                             rd=[f"pb{nbk}", f"rec4{c % 2}"], wr=[f"att.{t_}"])

            for b in range(10):
                if b < 8:
                    for hh in range(2):
                        emit_xw_batch(b, hh, qv, qkey)
                if 1 <= b <= 8:
                    c = b - 1
                    for hh in range(2):
                        nbk = (2 + c % 2) if hh == 0 else (6 + c % 2)
                        for j in range(2):
                            pst, pk = phis(c, hh, j)
                            for mc in range(2):
                                trout = pb[nbk].ap()[
                                    :, 256 + j * 128 + mc * 64:
                                    256 + j * 128 + (mc + 1) * 64].bitcast(BF16)
                                tr(trout, pst[:, mc * 128:(mc + 1) * 128],
                                   idb.ap(), rd=[pk, "idb"], wr=[f"pb{nbk}"])
                    for hh in range(2):
                        nbk = (2 + c % 2) if hh == 0 else (6 + c % 2)
                        for j in range(2):
                            pfm, fk = pfms(c, hh, j)
                            E.op("act", lambda e, o=pfm,
                                 i=pb[nbk].ap()[:, 256 + j * 128:
                                                384 + j * 128].bitcast(BF16):
                                 e.activation(o, i, AF.Copy,
                                              bias=EPS_PHI / 16.0, scale=1.0),
                                 rd=[f"pb{nbk}"], wr=[fk])
                    for hh in range(2):
                        nbk = (2 + c % 2) if hh == 0 else (6 + c % 2)
                        for j in range(2):
                            pfm, fk = pfms(c, hh, j)
                            for mc in range(2):
                                mm(pb[nbk].ap()[:, j * 65:j * 65 + 65],
                                   pfm[:, mc * 128:(mc + 1) * 128],
                                   kv_sb[hh][:, mc * 65:(mc + 1) * 65],
                                   mc == 0, mc == 1,
                                   rd=[fk, f"kv{hh}"], wr=[f"pb{nbk}"],
                                   tag="num")
                if b >= 2:
                    q_tail(b - 2)
                if b < 8:
                    emit_phi_batch(b, qnq, "qnq")

    def wo_phase_units():
        bobc = ar1.ap()[:, QK_OFF + 2 * T:QK_OFF + 2 * T + 2048].bitcast(F32)

        def wv_load_wot(dc):
            for hf in range(2):
                dma(wbuf[dc % 2].ap()[:, hf * 512:(hf + 1) * 512],
                    wot_d[dc * 128:(dc + 1) * 128,
                            hf * 512:(hf + 1) * 512],
                    [], ["wbuf%d%s" % (dc % 2, "ab"[hf])])

        def pro():
            for dc in range(2):
                wv_load_wot(dc)
            for dc in range(DC):
                r = dc % 2
                cp("act", wv_v(dc), wbuf[r].ap(),
                   rd=["wbuf%da" % r, "wbuf%db" % r],
                   wr=[f"wv.{dc}", "qec1", "kec1"] +
                      [f"phisl{i}" for i in range(16)])
                if dc + 2 < DC:
                    wv_load_wot(dc + 2)
            dma(borow, bo_d.unsqueeze(0), ["qec0"], ["borow"])
            for half in range(2):
                mm(pb[6 + half].ap(), onesf.ap(),
                   borow[0:1, half * 512:(half + 1) * 512], True, True,
                   rd=["borow", "onesf"], wr=[f"pb{6 + half}"])
                cp("dve", bobc[:, half * 512:(half + 1) * 512],
                   pb[6 + half].ap(),
                   rd=[f"pb{6 + half}"],
                   wr=["bobc"] + [f"qsqseg{i}" for i in range(8)])

        def wo_tau(tau):
            wb = [2, 3] if tau % 2 == 0 else [4, 5]
            idlep = [4, 5] if tau % 2 == 0 else [2, 3]
            trb = [6, 7, idlep[0], idlep[1]]
            for ec in range(DC):
                bk6 = trb[ec % 4]
                r = ec % 2
                tr(pbf(bk6), att_v(tau, ec * 128, (ec + 1) * 128), idb.ap(),
                   rd=[f"att.{tau}", "idb"], wr=[f"pb{bk6}"])
                cp("act", att_st[r], pbf(bk6),
                   rd=[f"pb{bk6}"], wr=[f"attst{r}"])
                for half in range(2):
                    mm(pb[wb[half]].ap(), att_st[r],
                       wv_v(ec)[:, half * 512:(half + 1) * 512],
                       ec == 0, ec == DC - 1,
                       rd=[f"attst{r}", f"wv.{ec}"], wr=[f"pb{wb[half]}"])
            for half in range(2):
                E.op("dve", lambda e, p=pb[wb[half]].ap(),
                     bb=bobc[:, half * 512:(half + 1) * 512]:
                     e.tensor_tensor(p, p, bb, ALU.add),
                     rd=[f"pb{wb[half]}", "bobc"], wr=[f"pb{wb[half]}"])
                xv = x_sb.ap()[:, tau * D + half * 512:
                               tau * D + half * 512 + 512]
                E.op("dve", lambda e, xv=xv, p=pb[wb[half]].ap():
                     e.scalar_tensor_tensor(xv, p, 1.0, xv,
                                            ALU.mult, ALU.add),
                     rd=[f"pb{wb[half]}", f"x.{tau}"], wr=[f"x.{tau}"])
        return pro, wo_tau

    if DO_SUB1:
        ffn(first=True)
    wo_units = None
    if DO_ATTN:
        E.fence()
        attention()
        wo_units = wo_phase_units()
        if not DO_SUB3:
            pro, wo_tau = wo_units
            pro()
            for tau in range(NTAU):
                wo_tau(tau)
    if DO_SUB3:
        ffn(first=False, wo=wo_units)

    o_src = x_sb.ap().rearrange("p (t d) -> p t d", d=D)
    o_dst = out_d.rearrange("(t p) d -> p t d", p=128)
    for tau in range(NTAU):
        dma(o_dst[:, tau:tau + 1, :], o_src[:, tau:tau + 1, :],
            [f"x.{tau}"], ["out"], tag="store")
    E.op("sp", lambda e: e.nop(), rd=["out"], tag="final")

    E.emit()
    return nc


def host_layout(L0_attn_x, L0_attn_Wq, L0_attn_Wk, L0_attn_Wv, L0_attn_Wo,
                L0_attn_bo, L0_attn_proj, L0_ffn_w1, L0_ffn_b1, L0_ffn_w2,
                L0_ffn_b2):
    import ml_dtypes
    x = np.ascontiguousarray(np.asarray(L0_attn_x, dtype=np.float32))
    w1 = np.asarray(L0_ffn_w1, dtype=np.float32)
    w2 = np.asarray(L0_ffn_w2, dtype=np.float32)
    wq = np.asarray(L0_attn_Wq, dtype=np.float32)
    wk = np.asarray(L0_attn_Wk, dtype=np.float32)
    wv = np.asarray(L0_attn_Wv, dtype=np.float32)
    wo = np.asarray(L0_attn_Wo, dtype=np.float32)
    proj = np.asarray(L0_attn_proj, dtype=np.float32)
    b1 = np.asarray(L0_ffn_b1, dtype=np.float32)
    b2 = np.asarray(L0_ffn_b2, dtype=np.float32)
    bo = np.asarray(L0_attn_bo, dtype=np.float32)

    w1s = np.ascontiguousarray(w1.T.reshape(D, HC, 128).transpose(1, 0, 2))
    w2t = np.ascontiguousarray(w2.T)
    wqs = np.ascontiguousarray(wq.T.reshape(D, DC, 128).transpose(1, 0, 2))
    wks = np.ascontiguousarray(wk.T.reshape(D, DC, 128).transpose(1, 0, 2))
    wvt = np.ascontiguousarray(wv.T)
    wot = np.ascontiguousarray(wo.T)
    projp = np.zeros((NH, 128, M), np.float32)
    for h in range(NH):
        r0 = 64 * (h % 2)
        projp[h, r0:r0 + 64, :] = proj[h].T
    b1t = np.ascontiguousarray(b1.reshape(HC, 128).T)
    idb = np.eye(128).astype(ml_dtypes.bfloat16)
    idf = np.eye(2).astype(np.float32)
    onesb = np.ones(128).astype(ml_dtypes.bfloat16)
    onesf = np.ones(128).astype(np.float32)
    blk2 = np.zeros((128, 2), np.float32)
    blk2[0:64, 0] = 1
    blk2[64:128, 1] = 1
    blk2 = blk2.astype(ml_dtypes.bfloat16)
    shared = {"w1s": w1s, "w2t": w2t, "wqs": wqs, "wks": wks, "wvt": wvt,
              "wot": wot, "projp": projp, "b1t": b1t, "b2": b2, "bo": bo,
              "idb": idb, "idf": idf, "onesb": onesb, "onesf": onesf, "blk2": blk2}
    return [dict(shared, x=np.ascontiguousarray(x[b])) for b in range(8)]


def kernel(**inputs):
    in_maps = host_layout(**inputs)
    nc = build_core_program()
    res = run_bass_kernel_spmd(nc, in_maps, core_ids=list(range(8)))
    out = np.stack([np.asarray(res.results[b]["out"]) for b in range(8)],
                   axis=0)
    return out.astype(np.float32)







# version 38 (final)
# speedup vs baseline: 1.0543x; 1.0543x over previous
import numpy as np
import concourse.bass as bass
import concourse.mybir as mybir
from concourse.bass_utils import run_bass_kernel_spmd

F32 = mybir.dt.float32
BF16 = mybir.dt.bfloat16
AF = mybir.ActivationFunctionType
ALU = mybir.AluOpType
AX = mybir.AxisListType

T = 2048
D = 1024
HID = 4096
NH = 16
DK = 64
M = 256
NTAU = 16
TC = 512
NTC = T // TC
HC = HID // 128
DC = D // 128
LN16 = float(np.log(16.0))
EPS_PHI = 1e-6

DO_SUB1 = True
DO_ATTN = True
DO_SUB3 = True

ENGINES = ("pe", "act", "dve", "pool", "sp")


class Emitter:

    NSP = 12

    def __init__(self, nc):
        self.nc = nc
        self.ops = []
        self.last_writer = {}
        self.readers = {}
        self.fence_set = set()
        self.sems = {e: nc.alloc_semaphore(f"esem_{e}")
                     for e in ("pe", "act", "dve", "pool")}
        for k in range(self.NSP):
            self.sems[f"sp{k}"] = nc.alloc_semaphore(f"esem_sp{k}")
        self.n_sp = 0

    def op(self, eng, fn, rd=(), wr=(), tag=""):
        o = {"eng": eng, "fn": fn, "deps": set(), "inc": False,
             "tag": tag, "idx": len(self.ops)}
        if eng == "sp":
            k = self.n_sp % self.NSP
            o["sem_eng"] = f"sp{k}"
            o["inc"] = True
            o["ring_prev"] = (self.n_sp - self.NSP
                              if self.n_sp >= self.NSP else None)
            self.n_sp += 1
        else:
            o["sem_eng"] = eng
        deps = o["deps"]

        def add_dep(p):
            if p is None or p is o:
                return
            if p["eng"] == "pe" and eng == "pe":
                return
            deps.add(p["idx"])

        for r in rd:
            add_dep(self.last_writer.get(r))
        for w in wr:
            add_dep(self.last_writer.get(w))
            for p in self.readers.get(w, {}).values():
                add_dep(p)
        for d in self.fence_set:
            if d < o["idx"]:
                p = self.ops[d]
                if not (p["eng"] == "pe" and eng == "pe"):
                    deps.add(d)
        for r in rd:
            self.readers.setdefault(r, {})[eng] = o
        for w in wr:
            self.last_writer[w] = o
            self.readers[w] = {}
        self.ops.append(o)
        return o

    def fence(self):
        s = set()
        for p in self.last_writer.values():
            s.add(p["idx"])
        for lst in self.readers.values():
            for p in lst.values():
                s.add(p["idx"])
        self.fence_set = s

    def emit(self):
        nc = self.nc
        ops = self.ops
        sp_ops = [o for o in ops if o["eng"] == "sp"]
        for o in ops:
            for d in o["deps"]:
                ops[d]["inc"] = True
        sem_names = list(self.sems.keys())
        cnt = {e: 0 for e in sem_names}
        for o in ops:
            se = o["sem_eng"]
            if o["inc"]:
                cnt[se] += 16 if o["eng"] == "sp" else 1
            o["inc_val"] = cnt[se]
        clocks = {e: {f: 0 for f in sem_names} for e in
                  ("pe", "act", "dve", "pool", "sp")}
        snaps = {}
        for o in ops:
            e = o["eng"]
            ce = clocks[e]
            need = {}
            if o.get("ring_prev") is not None:
                p = sp_ops[o["ring_prev"]]
                need[p["sem_eng"]] = p["inc_val"]
            for d in o["deps"]:
                p = ops[d]
                v = p["inc_val"]
                if v > need.get(p["sem_eng"], 0):
                    need[p["sem_eng"]] = v
            waits = []
            for f, v in sorted(need.items()):
                if ce[f] < v:
                    waits.append((f, v))
                    ce[f] = v
                    for g, gv in snaps[(f, v)].items():
                        if ce[g] < gv:
                            ce[g] = gv
            o["waits"] = waits
            if o["inc"]:
                snap = dict(ce)
                snap[o["sem_eng"]] = o["inc_val"]
                snaps[(o["sem_eng"], o["inc_val"])] = snap
        streams = {e: [] for e in ("pe", "act", "dve", "pool", "sp")}
        for o in ops:
            streams[o["eng"]].append(o)
        sems = self.sems

        def run_stream(eng_obj, eng_name):
            step = 16 if eng_name == "sp" else 1
            for o in streams[eng_name]:
                for f, v in o["waits"]:
                    eng_obj.wait_ge(sems[f], v)
                inst = o["fn"](eng_obj)
                if o["inc"]:
                    inst.then_inc(sems[o["sem_eng"]], step)

        with nc.Block() as block:
            @block.tensor
            def _(te):
                run_stream(te, "pe")

            @block.scalar
            def _(sc):
                run_stream(sc, "act")

            @block.vector
            def _(ve):
                run_stream(ve, "dve")

            @block.gpsimd
            def _(gp):
                run_stream(gp, "pool")

            @block.sync
            def _(sy):
                run_stream(sy, "sp")


def build_core_program():
    nc = bass.Bass("TRN2", target_bir_lowering=False)

    x_d = nc.dram_tensor("x", [T, D], F32, kind="ExternalInput").ap()
    w1s_d = nc.dram_tensor("w1s", [HC, D, 128], F32, kind="ExternalInput").ap()
    w2t_d = nc.dram_tensor("w2t", [HID, D], F32, kind="ExternalInput").ap()
    wqs_d = nc.dram_tensor("wqs", [DC, D, 128], F32, kind="ExternalInput").ap()
    wks_d = nc.dram_tensor("wks", [DC, D, 128], F32, kind="ExternalInput").ap()
    wvt_d = nc.dram_tensor("wvt", [D, D], F32, kind="ExternalInput").ap()
    wot_d = nc.dram_tensor("wot", [D, D], F32, kind="ExternalInput").ap()
    proj_d = nc.dram_tensor("projp", [NH, 128, M], F32, kind="ExternalInput").ap()
    b1t_d = nc.dram_tensor("b1t", [128, HC], F32, kind="ExternalInput").ap()
    b2_d = nc.dram_tensor("b2", [D], F32, kind="ExternalInput").ap()
    bo_d = nc.dram_tensor("bo", [D], F32, kind="ExternalInput").ap()
    idb_d = nc.dram_tensor("idb", [128, 128], BF16, kind="ExternalInput").ap()
    idf_d = nc.dram_tensor("idf", [2, 2], F32, kind="ExternalInput").ap()
    onesb_d = nc.dram_tensor("onesb", [128], BF16, kind="ExternalInput").ap()
    onesf_d = nc.dram_tensor("onesf", [128], F32, kind="ExternalInput").ap()
    blk2_d = nc.dram_tensor("blk2", [128, 2], BF16, kind="ExternalInput").ap()
    out_d = nc.dram_tensor("out", [T, D], F32, kind="ExternalOutput").ap()
    sgn1c_d = nc.dram_tensor("sgn1c", [HC, 128, D], BF16, kind="Internal").ap()
    bw2c_d = nc.dram_tensor("bw2c", [128, HC * D], BF16, kind="Internal").ap()
    sv_d = nc.dram_tensor("sv_dram", [1, 1040], F32, kind="Internal").ap()

    A = nc.alloc_sbuf_tensor
    x_sb = A("x_sb", [128, NTAU * D], F32)
    uT = A("uT", [128, 16384], BF16)
    ar1 = A("ar1", [128, 32768], BF16)
    ar2 = A("ar2", [128, 16384], BF16)
    wbuf = [A(f"wbuf{i}", [128, 1024], F32) for i in range(2)]
    wbf = [A(f"wbf{i}", [128, 1024], BF16) for i in range(2)]
    idb = A("idb_s", [128, 128], BF16)
    idf = A("idf_s", [2, 2], F32)
    onesb = A("onesb_s", [1, 128], BF16)
    onescol = A("onescol", [128, 1], BF16)
    onesf = A("onesf_s", [1, 128], F32)
    blk2 = A("blk2_s", [128, 2], BF16)
    b1sb = A("b1sb", [128, HC], F32)
    alpha1 = A("alpha1", [128, HC], F32)
    epsln = A("epsln", [128, 1], F32)
    lnt = A("lnt", [128, 1], F32)
    stats = [A(f"stats{i}", [128, 12], F32) for i in range(2)]
    mv = [A(f"mv{i}", [128, 2], F32) for i in range(2)]
    sd = [A(f"sd{i}", [128, 1], F32) for i in range(2)]
    rstd = [A(f"rstd{i}", [128, 1], F32) for i in range(2)]
    qnq = A("qnq", [128, 2 * NTAU], F32)
    qnk = A("qnk", [128, 2 * NTAU], F32)
    negmx4 = A("negmx4", [128, 16], F32)
    den4 = [A(f"den4_{i}", [128, 4], F32) for i in range(2)]
    rec4 = [A(f"rec4_{i}", [128, 4], F32) for i in range(2)]
    kv_sbA = A("kv_sbA", [128, 130], BF16)
    sve_pr = A("sve_pr", [1, 130], F32)
    csum_sb = A("csum_sb", [1, 130], F32)
    U1OFF = [0, 9472]
    U1OFF_B = 9472
    wabs = uT.ap()[:, 4096:5120]
    browF = uT.ap()[:, 5120:7168].bitcast(F32)[0:1, 0:1024]
    a1row = uT.ap()[:, 7168:7424].bitcast(F32)[0:1, 0:128]
    u_tmF = [uT.ap()[:, 7424 + i * 1024:7424 + (i + 1) * 1024]
             for i in range(2)]
    b2bc = uT.ap()[:, 13568:15616].bitcast(F32)
    w2buf = [uT.ap()[:, U1OFF_B + i * 2048:U1OFF_B + (i + 1) * 2048].bitcast(F32)
             for i in range(2)]
    a2bc = w2buf[0]
    u_tmA = [ar1.ap()[:, 24832 + i * 1024:24832 + (i + 1) * 1024]
             for i in range(2)]
    borow = None
    projb = [ar1.ap()[:, 30976 + i * M:30976 + (i + 1) * M] for i in range(2)]
    kvt_sb = [ar1.ap()[0:65, 31488 + i * M:31488 + (i + 1) * M]
              for i in range(2)]
    kv_sb2 = ar1.ap()[:, 32000:32130]
    kv_sb = [kv_sbA.ap(), kv_sb2]
    att_st = [ar1.ap()[:, 32130 + i * 128:32130 + (i + 1) * 128]
              for i in range(2)]

    def bw2_v(hc):
        return ar1.ap()[:, hc * D:(hc + 1) * D]
    V_COLS = NH * 65

    def v_v(tau, h):
        base = tau * V_COLS + h * 65
        return ar1.ap()[:, base:base + 65]
    WV_OFF = NTAU * V_COLS

    def wv_v(dc):
        return ar1.ap()[:, WV_OFF + dc * D: WV_OFF + (dc + 1) * D]
    QK_OFF = WV_OFF + DC * D
    borow = ar1.ap()[:, QK_OFF:QK_OFF + 2048].bitcast(F32)[0:1, 0:1024]
    q_ec = ar1.ap()[:, QK_OFF:QK_OFF + T]
    k_ec = ar1.ap()[:, QK_OFF + T:QK_OFF + 2 * T]
    qsq = ar1.ap()[:, QK_OFF + 2 * T:QK_OFF + 3 * T]

    def g_v(hc, t0, t1):
        return ar2.ap()[:, hc * TC + t0: hc * TC + t1]

    def att_v(tau, lo, hi):
        return ar2.ap()[:, tau * D + lo: tau * D + hi]

    pb = [nc.alloc_psum_tensor(f"pb{i}", [128, 512], F32) for i in range(8)]

    def pbf(i):
        return pb[i].ap()[:, 0:64].bitcast(BF16)

    E = Emitter(nc)

    def mm(out, lhsT, rhs, start, stop, rd, wr, tag=""):
        E.op("pe", lambda e, o=out, l=lhsT, r=rhs, s1=start, s2=stop:
             e.matmul(o, l, r, start=s1, stop=s2, skip_group_check=True),
             rd=rd, wr=wr, tag=tag)

    def tr(out, in_, ident, rd, wr, tag=""):
        E.op("pe", lambda e, o=out, i=in_, d=ident: e.transpose(o, i, d),
             rd=rd, wr=wr, tag=tag)

    def cp(eng, out, in_, rd, wr, tag=""):
        if eng == "act":
            E.op("act", lambda e, o=out, i=in_: e.copy(o, i),
                 rd=rd, wr=wr, tag=tag)
        else:
            E.op("dve", lambda e, o=out, i=in_: e.tensor_copy(o, i),
                 rd=rd, wr=wr, tag=tag)

    def dma(dst, src, rd, wr, tag=""):
        E.op("sp", lambda e, d=dst, s=src: e.dma_start(d, s),
             rd=rd, wr=wr, tag=tag)

    x_src = x_d.rearrange("(t p) d -> p t d", p=128)
    x_dst = x_sb.ap().rearrange("p (t d) -> p t d", d=D)
    for tc4 in range(4):
        dma(x_dst[:, tc4 * 4:(tc4 + 1) * 4, :],
            x_src[:, tc4 * 4:(tc4 + 1) * 4, :],
            [], [f"x.{t}" for t in range(tc4 * 4, tc4 * 4 + 4)], tag="ldx")
    for dst, src, key in [
        (idb.ap(), idb_d, "idb"), (idf.ap(), idf_d, "idf"),
        (onesb.ap()[0:1, :], onesb_d.unsqueeze(0), "onesb"),
        (onescol.ap()[:, 0:1], onesb_d.unsqueeze(1), "onescol"),
        (onesf.ap()[0:1, :], onesf_d.unsqueeze(0), "onesf"),
        (blk2.ap(), blk2_d, "blk2"), (b1sb.ap(), b1t_d, "b1sb"),
    ]:
        dma(dst, src, [], [key])
    E.op("dve", lambda e: e.memset(epsln.ap(), 1e-5), wr=["epsln"])
    E.op("dve", lambda e: e.memset(lnt.ap(), -LN16), wr=["lnt"])

    def ln_transpose(tau, col_of, key_of, utm, ukey, fill=None,
                     extra_wr=()):
        r = tau % 2
        xk = f"x.{tau}"
        xv = x_sb.ap()[:, tau * D:(tau + 1) * D]
        out = []
        out.append(lambda: E.op(
            "dve", lambda e, s=stats[r].ap()[:, 0:6], x=xv[:, 0:512]:
            e.bn_stats(s, x), rd=[xk], wr=[f"st{r}a"]))
        out.append(lambda: E.op(
            "dve", lambda e, s=stats[r].ap()[:, 6:12], x=xv[:, 512:1024]:
            e.bn_stats(s, x), rd=[xk], wr=[f"st{r}b"]))
        out.append(lambda: E.op(
            "dve", lambda e, o=mv[r].ap(), i=stats[r].ap(): e.bn_aggr(o, i),
            rd=[f"st{r}a", f"st{r}b"], wr=[f"mv{r}"]))
        out.append(lambda: E.op(
            "act", lambda e, o=sd[r].ap(), i=mv[r].ap()[:, 1:2]:
            e.activation(o, i, AF.Sqrt, bias=epsln.ap()),
            rd=[f"mv{r}", "epsln"], wr=[f"sd{r}"]))
        out.append(lambda: E.op(
            "dve", lambda e, o=rstd[r].ap(), i=sd[r].ap(): e.reciprocal(o, i),
            rd=[f"sd{r}"], wr=[f"rstd{r}"]))
        out.append(lambda: E.op(
            "dve", lambda e, o=utm[r], x=xv, m=mv[r].ap()[:, 0:1],
            s=rstd[r].ap():
            e.tensor_scalar(o, x, m, s, ALU.subtract, ALU.mult),
            rd=[xk, f"mv{r}", f"rstd{r}"],
            wr=[f"{ukey}{r}"] + list(extra_wr)))
        for dc in range(DC):
            bk = 6 + (dc % 2)

            def unit(dc=dc, bk=bk):
                tr(pbf(bk), utm[r][:, dc * 128:(dc + 1) * 128], idb.ap(),
                   rd=[f"{ukey}{r}", "idb"], wr=[f"pb{bk}"])
                col = col_of(dc)
                eng = "dve" if dc % 2 == 0 else "act"
                cp(eng, uT.ap()[:, col:col + 128], pbf(bk),
                   rd=[f"pb{bk}"], wr=[key_of(dc)] + list(extra_wr))
            out.append(unit)
        if fill is None:
            for f in out:
                f()
        else:
            fill.extend(out)

    def ffn(first, wo=None):
        bwkeys = [f"bw2.{hc}" for hc in range(HC)]
        bw2_part_b = None
        if first:
            def w2_load(hc):
                r = hc % 2
                for hf in range(2):
                    dma(w2buf[r][:, hf * 512:(hf + 1) * 512],
                        w2t_d[hc * 128:(hc + 1) * 128,
                              hf * 512:(hf + 1) * 512],
                        [], ["w2buf%d%s" % (r, "ab"[hf])])

            def w2_unit(hc):
                r = hc % 2
                k2 = ["w2buf%da" % r, "w2buf%db" % r]
                E.op("dve", lambda e, i=w2buf[r]: e.scalar_tensor_tensor(
                    wabs, i, -1.0, i, ALU.mult, ALU.max),
                    rd=k2, wr=["wabs"])
                E.op("act", lambda e, o=bw2_v(hc), i=w2buf[r]:
                     e.sign(o, i), rd=k2, wr=[f"bw2.{hc}"])
                if hc + 2 < HC:
                    w2_load(hc + 2)
                for half in range(2):
                    mm(pb[6 + half].ap()[0:1, 0:512], onescol.ap(),
                       wabs[:, half * 512:(half + 1) * 512],
                       hc == 0, hc == HC - 1,
                       rd=["wabs", "onescol"], wr=[f"pb{6 + half}"])
            for hc in range(2):
                w2_load(hc)
            w2fill = list(range(HC))
            def w2_finish():
                for half in range(2):
                    E.op("dve", lambda e,
                         o=browF[0:1, half * 512:(half + 1) * 512],
                         i=pb[6 + half].ap()[0:1, 0:512]:
                         e.tensor_scalar(o, i, 0.5 / HID, None, ALU.mult),
                         rd=[f"pb{6 + half}"], wr=["browF"])
                for half in range(2):
                    mm(pb[half].ap(), onesf.ap(),
                       browF[0:1, half * 512:(half + 1) * 512], True, True,
                       rd=["browF", "onesf"], wr=[f"pb{half}"])
                    cp("dve", a2bc[:, half * 512:(half + 1) * 512],
                       pb[half].ap(), rd=[f"pb{half}"], wr=["w2buf0"])
                for hc in range(HC):
                    E.op("dve", lambda e, o=bw2_v(hc): e.tensor_tensor(
                        o, o, a2bc, ALU.mult),
                        rd=[f"bw2.{hc}", "w2buf0"], wr=[f"bw2.{hc}"])
                dma(bw2c_d, ar1.ap()[:, 0:HC * D], bwkeys, ["bw2c"])
        else:
            vkeys = [f"v.{t}.{hf}" for t in range(NTAU) for hf in range(2)]
            restkeys = ([f"wv.{dc}" for dc in range(DC)]
                        + ["qec0", "qec1", "kec0", "kec1",
                           "u_tmA0", "u_tmA1"]
                        + [f"phisl{i}" for i in range(16)]
                        + [f"qsqseg{i}" for i in range(8)]
                        + ["kvt0", "kvt1", "kv0", "kv1", "qnq", "qnk",
                           "qn_st", "bobc", "vones", "borow",
                           "projb0", "projb1", "attst0", "attst1"])
            if wo is None:
                dma(ar1.ap()[:, 0:HC * D], bw2c_d, ["bw2c"],
                    bwkeys + vkeys + restkeys)
            else:
                dma(ar1.ap()[:, 0:16 * D], bw2c_d[:, 0:16 * D], ["bw2c"],
                    bwkeys[:16] + vkeys + ["vones"])

                def bw2_part_b():
                    dma(ar1.ap()[:, 16 * D:HC * D], bw2c_d[:, 16 * D:HC * D],
                        ["bw2c"], bwkeys[16:] + vkeys[30:] + restkeys)
        def b2_setup():
            dma(browF, b2_d.unsqueeze(0), ["browF"],
                ["browF"] if first else ["browF", "uTall"])
            E.op("dve", lambda e: e.tensor_scalar(browF, browF, 0.5, None,
                                                  ALU.mult),
                 rd=["browF"], wr=["browF"])
            for half in range(2):
                mm(pb[6 + half].ap(), onesf.ap(),
                   browF[0:1, half * 512:(half + 1) * 512], True, True,
                   rd=["browF", "onesf"], wr=[f"pb{6 + half}"])
                cp("dve", b2bc[:, half * 512:(half + 1) * 512],
                   pb[6 + half].ap(), rd=[f"pb{6 + half}"],
                   wr=["b2bc"] if first else ["b2bc", "uTall"])
        if not first:
            b2_setup()

        def emit_ln(tc, fill=None):
            pbuf = tc % 2
            extra = ["w2buf0", "w2buf1"] if (first and tc == 1) else []
            if not first:
                extra = extra + ["uTall"]
            for tau in range(tc * 4, tc * 4 + 4):
                j = tau - tc * 4
                ln_transpose(tau,
                             lambda dc, j=j, p=pbuf: U1OFF[p] + dc * TC + j * 128,
                             lambda dc, j=j, p=pbuf: f"u1T.{p}.{dc}.{j}",
                             u_tmF, "u_tmF", fill=fill, extra_wr=extra)

        wo_next = 0

        def wo_ensure(upto):
            nonlocal wo_next
            while wo_next <= min(upto, NTAU - 1):
                wo[1](wo_next)
                wo_next += 1

        if wo is not None:
            wo[0]()
            wo_ensure(3)
        emit_ln(0)
        for tc in range(NTC):
            pbuf = tc % 2
            u1o = U1OFF[pbuf]
            stream_fp32 = first and tc == 0

            def w1_load(hc):
                r = hc % 2
                if stream_fp32:
                    for hf in range(2):
                        dma(wbuf[r].ap()[:, hf * 512:(hf + 1) * 512].rearrange(
                            "p (dc h) -> p dc h", h=128),
                            w1s_d[hc, hf * 512:(hf + 1) * 512, :].rearrange(
                                "(dc p) h -> p dc h", p=128),
                            [], ["wbuf%d%s" % (r, "ab"[hf])])
                else:
                    for hf in range(2):
                        dma(wbf[r].ap()[:, hf * 512:(hf + 1) * 512],
                            sgn1c_d[hc, :, hf * 512:(hf + 1) * 512],
                            ["sgn1c"], ["wbf%d%s" % (r, "ab"[hf])])
            for hc in range(2):
                w1_load(hc)
            for hc in range(HC):
                r = hc % 2
                if wo is not None and tc == 0:
                    wo_ensure(hc // 2 + 4)
                if stream_fp32:
                    E.op("dve", lambda e, i=wbuf[r].ap(): e.scalar_tensor_tensor(
                        wabs, i, -1.0, i, ALU.mult, ALU.max),
                        rd=["wbuf%da" % r, "wbuf%db" % r], wr=["wabs"])
                    E.op("act", lambda e, o=wbf[r].ap(), i=wbuf[r].ap():
                         e.sign(o, i), rd=["wbuf%da" % r, "wbuf%db" % r],
                         wr=["wbf%da" % r, "wbf%db" % r])
                    for dc in range(DC):
                        mm(pb[5].ap()[0:1, 0:128], onescol.ap(),
                           wabs[:, dc * 128:(dc + 1) * 128],
                           dc == 0, dc == DC - 1,
                           rd=["wabs", "onescol"], wr=["pb5"])
                    E.op("dve", lambda e, o=a1row[0:1, :],
                         i=pb[5].ap()[0:1, 0:128]:
                         e.tensor_scalar(o, i, 1.0 / D, None, ALU.mult),
                         rd=["pb5"], wr=["a1row"])
                    tr(pb[4].ap()[:, 0:1], a1row[0:1, :],
                       idf.ap()[0:1, 0:1], rd=["a1row", "idf"], wr=["pb4"])
                    cp("dve", alpha1.ap()[:, hc:hc + 1], pb[4].ap()[:, 0:1],
                       rd=["pb4"], wr=[f"alpha1.{hc}"])
                    dma(sgn1c_d[hc], wbf[r].ap(), ["wbf%da" % r, "wbf%db" % r], ["sgn1c"])
                hb = hc % 2
                for dc in range(DC):
                    mm(pb[hb].ap(), wbf[r].ap()[:, dc * 128:(dc + 1) * 128],
                       uT.ap()[:, u1o + dc * TC:u1o + (dc + 1) * TC],
                       dc == 0, dc == DC - 1,
                       rd=["wbf%d%s" % (r, "ab"[dc // 4])] +
                          [f"u1T.{pbuf}.{dc}.{j}" for j in range(4)],
                       wr=[f"pb{hb}"])
                if hc + 2 < HC:
                    w1_load(hc + 2)
                E.op("act", lambda e, o=g_v(hc, 0, TC), i=pb[hb].ap(),
                     b=b1sb.ap()[:, hc:hc + 1], s=alpha1.ap()[:, hc:hc + 1]:
                     e.activation(o, i, AF.Gelu, bias=b, scale=s),
                     rd=[f"pb{hb}", f"alpha1.{hc}", "b1sb"],
                     wr=[f"g.{hc}"] if first else
                        [f"g.{hc}", f"att.{hc // 2}"])
                if stream_fp32:
                    w2_unit(hc)
                    if hc == HC - 1:
                        w2_finish()
                        b2_setup()
            if wo is not None and tc == 0:
                wo_ensure(NTAU - 1)
                bw2_part_b()
            lnfill = []
            if tc + 1 < NTC:
                emit_ln(tc + 1, fill=lnfill)
            li = 0
            for tp in range(2):
                banks = [2, 3, 4, 5]
                for hc in range(HC):
                    for jj in range(4):
                        ti = tp * 2 + jj // 2
                        half = jj % 2
                        bk = banks[jj]
                        mm(pb[bk].ap(), g_v(hc, ti * 128, ti * 128 + 128),
                           bw2_v(hc)[:, half * 512:(half + 1) * 512],
                           hc == 0, False,
                           rd=[f"g.{hc}", f"bw2.{hc}"], wr=[f"pb{bk}"])
                    if li < len(lnfill):
                        lnfill[li]()
                        li += 1
                    if tp == 1 and hc >= HC - 4:
                        while li < len(lnfill):
                            lnfill[li]()
                            li += 1
                for jj in range(4):
                    ti = tp * 2 + jj // 2
                    half = jj % 2
                    bk = banks[jj]
                    E.op("dve", lambda e, p=pb[bk].ap(),
                         bb=b2bc[:, half * 512:(half + 1) * 512]:
                         e.tensor_tensor(p, p, bb, ALU.add),
                         rd=[f"pb{bk}", "b2bc"], wr=[f"pb{bk}"])
                    tau = tc * 4 + ti
                    xv = x_sb.ap()[:, tau * D + half * 512:
                                   tau * D + half * 512 + 512]
                    E.op("dve", lambda e, xv=xv, p=pb[bk].ap():
                         e.scalar_tensor_tensor(xv, p, 1.0, xv,
                                                ALU.mult, ALU.add),
                         rd=[f"pb{bk}", f"x.{tau}"], wr=[f"x.{tau}"])

    def attention():
        def v_tau3(tau):
            base = tau * V_COLS
            return ar1.ap()[:, base:base + V_COLS].rearrange(
                "p (h c) -> p h c", c=65)
        lnf = []
        for tau in range(NTAU):
            ln_transpose(tau, lambda dc, tau=tau: dc * T + tau * 128,
                         lambda dc, tau=tau: f"u2T.{dc}.{tau}", u_tmA, "u_tmA",
                         fill=lnf)
        def wv_load_wvt(dc):
            for hf in range(2):
                dma(wbuf[dc % 2].ap()[:, hf * 512:(hf + 1) * 512],
                    wvt_d[dc * 128:(dc + 1) * 128,
                            hf * 512:(hf + 1) * 512],
                    [], ["wbuf%d%s" % (dc % 2, "ab"[hf])])
        for dc in range(2):
            wv_load_wvt(dc)
        for dc in range(DC):
            r = dc % 2
            cp("act", wv_v(dc), wbuf[r].ap(),
               rd=["wbuf%da" % r, "wbuf%db" % r], wr=[f"wv.{dc}"])
            if dc + 2 < DC:
                wv_load_wvt(dc + 2)
        NU = len(lnf) // NTAU
        for f in lnf[0:2 * NU]:
            f()
        for tau in range(NTAU):
            for f in lnf[(tau + 2) * NU:(tau + 3) * NU]:
                f()
            for half in range(2):
                bk = (tau * 2 + half) % 2
                for dc in range(DC):
                    mm(pb[bk].ap(),
                       uT.ap()[:, dc * T + tau * 128:dc * T + tau * 128 + 128],
                       wv_v(dc)[:, half * 512:(half + 1) * 512],
                       dc == 0, dc == DC - 1,
                       rd=[f"wv.{dc}", f"u2T.{dc}.{tau}", "uTall"],
                       wr=[f"pb{bk}"])
                dst = v_tau3(tau)[:, half * 8:half * 8 + 8, 0:64]
                srcv = pb[bk].ap().rearrange("p (h c) -> p h c", c=64)
                E.op("act", lambda e, d=dst, s=srcv: e.copy(d, s),
                     rd=[f"pb{bk}"], wr=[f"v.{tau}.{half}"])
            E.op("dve", lambda e, v=v_tau3(tau)[:, :, 64:65]: e.memset(v, 1.0),
                 wr=["vones"])

        for g4 in range(4):
            bkv = [2, 3, 6, 7][g4]
            for t_ in range(NTAU):
                vb = t_ * V_COLS + g4 * 260
                mm(pb[bkv].ap()[0:1, 0:260], onescol.ap(),
                   ar1.ap()[:, vb:vb + 260], t_ == 0, t_ == NTAU - 1,
                   rd=[f"v.{t_}.{g4 // 2}", "vones", "onescol"],
                   wr=[f"pb{bkv}"], tag="sv")
        svbufs = [(sve_pr, "svepr0"), (csum_sb, "svepr1")]
        for pr8 in range(8):
            bkv = [2, 3, 6, 7][pr8 // 2]
            stg, skey = svbufs[pr8 % 2]
            E.op("act", lambda e, o=stg.ap()[0:1, 0:130],
                 i=pb[bkv].ap()[0:1, (pr8 % 2) * 130:(pr8 % 2) * 130 + 130]:
                 e.activation(o, i, AF.Copy, bias=0.0, scale=EPS_PHI / 16.0),
                 rd=[f"pb{bkv}"], wr=[skey])
            dma(sv_d[0:1, pr8 * 130:(pr8 + 1) * 130], stg.ap()[0:1, 0:130],
                [skey], [f"svdram{pr8}"])

        qk_buf = [(q_ec, k_ec, "qec0", "kec0"),
                  (ar1.ap()[:, WV_OFF:WV_OFF + T],
                   ar1.ap()[:, WV_OFF + T:WV_OFF + 2 * T], "qec1", "kec1")]

        def emit_qkv(p, fill=None):
            qv, kv, qkey, kkey = qk_buf[p % 2]
            wv_war = ((["wv.0", "wv.1"], ["wv.2", "wv.3"]) if p % 2 == 1
                      else (["u_tmA0", "u_tmA1"], []))
            out = []
            for w_i, wsl, dst, dkey, war in [(0, wqs_d, qv, qkey, wv_war[0]),
                                             (1, wks_d, kv, kkey, wv_war[1])]:
                def qk_load(w_i=w_i, wsl=wsl):
                    for hf in range(2):
                        dma(wbuf[w_i].ap()[:, hf * 512:(hf + 1) * 512]
                            .rearrange("p (dc h) -> p dc h", h=128),
                            wsl[p, hf * 512:(hf + 1) * 512, :].rearrange(
                                "(dc p) h -> p dc h", p=128),
                            [], ["wbuf%d%s" % (w_i, "ab"[hf])])
                out.append(qk_load)
                out.append(lambda w_i=w_i: E.op(
                    "dve", lambda e, o=wbf[w_i].ap(), i=wbuf[w_i].ap():
                    e.tensor_copy(o, i),
                    rd=["wbuf%da" % w_i, "wbuf%db" % w_i],
                    wr=[f"wbf{w_i}"]))
                for tcq in range(4):
                    bk = 6 + tcq % 2 if fill is not None else tcq % 2
                    def grp(w_i=w_i, tcq=tcq, bk=bk, dst=dst, dkey=dkey,
                            war=war):
                        for dc in range(DC):
                            mm(pb[bk].ap(),
                               wbf[w_i].ap()[:, dc * 128:(dc + 1) * 128],
                               uT.ap()[:, dc * T + tcq * 512:
                                       dc * T + tcq * 512 + 512],
                               dc == 0, dc == DC - 1,
                               rd=[f"wbf{w_i}", "uTall"] +
                                  [f"u2T.{dc}.{tcq * 4 + j}"
                                   for j in range(4)],
                               wr=[f"pb{bk}"])
                        E.op("act", lambda e,
                             o=dst[:, tcq * 512:(tcq + 1) * 512],
                             i=pb[bk].ap():
                             e.activation(o, i, AF.Copy, bias=0.0,
                                          scale=0.125),
                             rd=[f"pb{bk}"], wr=[dkey] + war)
                    out.append(grp)
            if fill is None:
                for f in out:
                    f()
            else:
                fill.extend(out)

        emit_qkv(0)
        segk = [f"qsqseg{i}" for i in range(8)]
        for pr in range(8):
            qv, kv, qkey, kkey = qk_buf[pr % 2]
            PHI_OFF = WV_OFF + 4096
            qnst = [ar1.ap()[:, PHI_OFF + (8 + 4 * i) * M:
                             PHI_OFF + (12 + 4 * i) * M].bitcast(F32)[0:2, :]
                    for i in range(2)]
            qnst_k = [[f"phisl{8 + i}" for i in range(4)],
                      [f"phisl{12 + i}" for i in range(4)]]
            for src_, skey, qdst, qk_ in [(qv, qkey, qnq, "qnq"),
                                          (kv, kkey, qnk, "qnk")]:
                E.op("act", lambda e, i=src_: e.activation(qsq, i, AF.Square),
                     rd=[skey], wr=segk)
                for g2 in range(6):
                    if g2 < 4:
                        qb = 4 + (g2 % 2)
                        mm(pb[qb].ap()[0:2, 0:512], blk2.ap(),
                           qsq[:, g2 * 512:(g2 + 1) * 512], True, True,
                           rd=[segk[2 * g2], segk[2 * g2 + 1], "blk2"],
                           wr=[f"pb{qb}"])
                        E.op("dve", lambda e, o=qnst[g2 % 2],
                             i=pb[qb].ap()[0:2, 0:512]:
                             e.tensor_copy(o, i),
                             rd=[f"pb{qb}"], wr=qnst_k[g2 % 2])
                    if g2 >= 2:
                        g = g2 - 2
                        dst2 = qdst.ap().rearrange(
                            "p (h t) -> p h t", t=NTAU)[:, :, 4 * g:4 * g + 4]
                        for j in range(4):
                            cp("dve",
                               dst2[:, :, j:j + 1],
                               pb[6 + j % 2].ap()[:, j * 2:j * 2 + 2]
                               .unsqueeze(2),
                               rd=[f"pb{6 + j % 2}"], wr=[qk_])
                    if 1 <= g2 <= 4:
                        g = g2 - 1
                        for j in range(4):
                            tr(pb[6 + j % 2].ap()[:, j * 2:j * 2 + 2],
                               qnst[g % 2][0:2, j * 128:(j + 1) * 128],
                               idf.ap(),
                               rd=qnst_k[g % 2] + ["idf"],
                               wr=[f"pb{6 + j % 2}"])
                E.op("act", lambda e, o=qdst.ap(): e.activation(
                    o, o, AF.Copy, bias=-LN16, scale=-0.5),
                    rd=[qk_], wr=[qk_])
            svb, svkey = svbufs[pr % 2]
            dma(svb.ap()[0:1, 0:130], sv_d[0:1, pr * 130:(pr + 1) * 130],
                [f"svdram{pr}"], [svkey])
            for hh in range(2):
                h = pr * 2 + hh
                dma(wbuf[hh].ap()[:, 0:M], proj_d[h], [],
                    ["wbuf%da" % hh])
                cp("act", projb[hh], wbuf[hh].ap()[:, 0:M],
                   rd=["wbuf%da" % hh], wr=[f"projb{hh}"])

            def phis_i(i):
                return (ar1.ap()[:, PHI_OFF + i * M:PHI_OFF + (i + 1) * M],
                        f"phisl{i}")

            def phis(b, hh, r):
                return phis_i((b % 4) * 4 + hh * 2 + r)

            def pfms(b, hh, r):
                return phis(b, hh, r)

            def emit_xw_batch(b, hh, src_ec, skey):
                bk = hh * 4 + b % 2
                for j in range(2):
                    tau = b * 2 + j
                    mm(pb[bk].ap()[:, j * 256:(j + 1) * 256],
                       src_ec[:, tau * 128:(tau + 1) * 128],
                       projb[hh], True, True,
                       rd=[skey, f"projb{hh}"], wr=[f"pb{bk}"])

            def emit_phi_batch(b, qn_t, qn_key):
                for hh in range(2):
                    bk = hh * 4 + b % 2
                    for j in range(2):
                        sl = (b % 4) * 4 + hh * 2 + j
                        xwv = pb[bk].ap()[:, j * 256:(j + 1) * 256]
                        E.op("dve", lambda e, o=negmx4.ap()[:, sl:sl + 1],
                             i=xwv:
                             e.reduce_max(o, i, AX.X, negate=True),
                             rd=[f"pb{bk}"], wr=[f"negmx{sl}"])
                for hh in range(2):
                    sl0 = (b % 4) * 4 + hh * 2
                    col = hh * NTAU + b * 2
                    E.op("dve", lambda e, o=negmx4.ap()[:, sl0:sl0 + 2],
                         i1=qn_t.ap()[:, col:col + 2]:
                         e.tensor_tensor(o, o, i1, ALU.add),
                         rd=[f"negmx{sl0}", f"negmx{sl0 + 1}", qn_key],
                         wr=[f"negmx{sl0}", f"negmx{sl0 + 1}"])
                for hh in range(2):
                    bk = hh * 4 + b % 2
                    for j in range(2):
                        pst, pk = phis(b, hh, j)
                        sl = (b % 4) * 4 + hh * 2 + j
                        xwv = pb[bk].ap()[:, j * 256:(j + 1) * 256]
                        E.op("act", lambda e, o=pst, i=xwv,
                             bb=negmx4.ap()[:, sl:sl + 1]:
                             e.activation(o, i, AF.Exp, bias=bb),
                             rd=[f"pb{bk}", f"negmx{sl}"], wr=[pk])

            fill = []
            if pr + 1 < 8:
                emit_qkv(pr + 1, fill=fill)
            fi = 0
            def do_fill(n):
                nonlocal fi
                for _ in range(n):
                    if fi < len(fill):
                        fill[fi]()
                        fi += 1
            for b in range(9):
                if b < 8:
                    for hh in range(2):
                        emit_xw_batch(b, hh, kv, kkey)
                if b >= 1:
                    c = b - 1
                    for hh in range(2):
                        h = pr * 2 + hh
                        for j in range(2):
                            t_ = c * 2 + j
                            pst, pk = phis(c, hh, j)
                            mm(pb[2 + hh].ap()[0:65, 0:256], v_v(t_, h), pst,
                               t_ == 0, False,
                               rd=[pk, f"v.{t_}.{h // 8}", "vones"],
                               wr=[f"pb{2 + hh}"], tag="kvt")
                do_fill(len(fill) if b == 3 else 0)
                if b < 8:
                    emit_phi_batch(b, qnk, "qnk")
            for hh in range(2):
                kvt_stg = kvt_sb[hh]
                for mh in range(2):
                    mm(pb[2 + hh].ap()[0:65, mh * 128:(mh + 1) * 128],
                       svb.ap()[0:1, hh * 65:(hh + 1) * 65],
                       onesf.ap()[0:1, 0:128], False, True,
                       rd=[svkey, "onesf"], wr=[f"pb{2 + hh}"], tag="kvt")
                cp("act", kvt_stg, pb[2 + hh].ap()[0:65, 0:256],
                   rd=[f"pb{2 + hh}"], wr=[f"kvt{hh}"])
                for mc in range(2):
                    bk6 = 6 + mc
                    tr(pbf(bk6)[:, 0:65],
                       kvt_stg[0:65, mc * 128:(mc + 1) * 128],
                       idb.ap()[0:65, 0:65],
                       rd=[f"kvt{hh}", "idb"], wr=[f"pb{bk6}"])
                    cp("dve", kv_sb[hh][:, mc * 65:(mc + 1) * 65],
                       pbf(bk6)[:, 0:65], rd=[f"pb{bk6}"], wr=[f"kv{hh}"])
            def q_tail(c):
                dn, rc = den4[c % 2], rec4[c % 2]
                for hh in range(2):
                    nbk = (2 + c % 2) if hh == 0 else (6 + c % 2)
                    for j in range(2):
                        sl = hh * 2 + j
                        E.op("dve", lambda e, o=dn.ap()[:, sl:sl + 1],
                             i=pb[nbk].ap()[:, j * 65 + 64:j * 65 + 65]:
                             e.tensor_scalar(o, i, EPS_PHI, None, ALU.add),
                             rd=[f"pb{nbk}"], wr=[f"den4{c % 2}"])
                E.op("dve", lambda e, o=rc.ap(), i=dn.ap():
                     e.reciprocal(o, i),
                     rd=[f"den4{c % 2}"], wr=[f"rec4{c % 2}"])
                for hh in range(2):
                    h = pr * 2 + hh
                    nbk = (2 + c % 2) if hh == 0 else (6 + c % 2)
                    for j in range(2):
                        t_ = c * 2 + j
                        sl = hh * 2 + j
                        E.op("dve", lambda e,
                             o=att_v(t_, h * 64, h * 64 + 64),
                             i=pb[nbk].ap()[:, j * 65:j * 65 + 64],
                             s=rc.ap()[:, sl:sl + 1]:
                             e.tensor_scalar(o, i, s, None, ALU.mult),
                             rd=[f"pb{nbk}", f"rec4{c % 2}"], wr=[f"att.{t_}"])

            for b in range(10):
                if b < 8:
                    for hh in range(2):
                        emit_xw_batch(b, hh, qv, qkey)
                if 1 <= b <= 8:
                    c = b - 1
                    for hh in range(2):
                        nbk = (2 + c % 2) if hh == 0 else (6 + c % 2)
                        for j in range(2):
                            pst, pk = phis(c, hh, j)
                            for mc in range(2):
                                trout = pb[nbk].ap()[
                                    :, 256 + j * 128 + mc * 64:
                                    256 + j * 128 + (mc + 1) * 64].bitcast(BF16)
                                tr(trout, pst[:, mc * 128:(mc + 1) * 128],
                                   idb.ap(), rd=[pk, "idb"], wr=[f"pb{nbk}"])
                    for hh in range(2):
                        nbk = (2 + c % 2) if hh == 0 else (6 + c % 2)
                        for j in range(2):
                            pfm, fk = pfms(c, hh, j)
                            E.op("act", lambda e, o=pfm,
                                 i=pb[nbk].ap()[:, 256 + j * 128:
                                                384 + j * 128].bitcast(BF16):
                                 e.activation(o, i, AF.Copy,
                                              bias=EPS_PHI / 16.0, scale=1.0),
                                 rd=[f"pb{nbk}"], wr=[fk])
                    for hh in range(2):
                        nbk = (2 + c % 2) if hh == 0 else (6 + c % 2)
                        for j in range(2):
                            pfm, fk = pfms(c, hh, j)
                            for mc in range(2):
                                mm(pb[nbk].ap()[:, j * 65:j * 65 + 65],
                                   pfm[:, mc * 128:(mc + 1) * 128],
                                   kv_sb[hh][:, mc * 65:(mc + 1) * 65],
                                   mc == 0, mc == 1,
                                   rd=[fk, f"kv{hh}"], wr=[f"pb{nbk}"],
                                   tag="num")
                if b >= 2:
                    q_tail(b - 2)
                if b < 8:
                    emit_phi_batch(b, qnq, "qnq")

    def wo_phase_units():
        bobc = ar1.ap()[:, QK_OFF + 2 * T:QK_OFF + 2 * T + 2048].bitcast(F32)

        def wv_load_wot(dc):
            for hf in range(2):
                dma(wbuf[dc % 2].ap()[:, hf * 512:(hf + 1) * 512],
                    wot_d[dc * 128:(dc + 1) * 128,
                            hf * 512:(hf + 1) * 512],
                    [], ["wbuf%d%s" % (dc % 2, "ab"[hf])])

        def pro():
            for dc in range(2):
                wv_load_wot(dc)
            for dc in range(DC):
                r = dc % 2
                cp("act", wv_v(dc), wbuf[r].ap(),
                   rd=["wbuf%da" % r, "wbuf%db" % r],
                   wr=[f"wv.{dc}", "qec1", "kec1"] +
                      [f"phisl{i}" for i in range(16)])
                if dc + 2 < DC:
                    wv_load_wot(dc + 2)
            dma(borow, bo_d.unsqueeze(0), ["qec0"], ["borow"])
            for half in range(2):
                mm(pb[6 + half].ap(), onesf.ap(),
                   borow[0:1, half * 512:(half + 1) * 512], True, True,
                   rd=["borow", "onesf"], wr=[f"pb{6 + half}"])
                cp("dve", bobc[:, half * 512:(half + 1) * 512],
                   pb[6 + half].ap(),
                   rd=[f"pb{6 + half}"],
                   wr=["bobc"] + [f"qsqseg{i}" for i in range(8)])

        def wo_tau(tau):
            wb = [2, 3] if tau % 2 == 0 else [4, 5]
            idlep = [4, 5] if tau % 2 == 0 else [2, 3]
            trb = [6, 7, idlep[0], idlep[1]]
            for ec in range(DC):
                bk6 = trb[ec % 4]
                r = ec % 2
                tr(pbf(bk6), att_v(tau, ec * 128, (ec + 1) * 128), idb.ap(),
                   rd=[f"att.{tau}", "idb"], wr=[f"pb{bk6}"])
                cp("act", att_st[r], pbf(bk6),
                   rd=[f"pb{bk6}"], wr=[f"attst{r}"])
                for half in range(2):
                    mm(pb[wb[half]].ap(), att_st[r],
                       wv_v(ec)[:, half * 512:(half + 1) * 512],
                       ec == 0, ec == DC - 1,
                       rd=[f"attst{r}", f"wv.{ec}"], wr=[f"pb{wb[half]}"])
            for half in range(2):
                E.op("dve", lambda e, p=pb[wb[half]].ap(),
                     bb=bobc[:, half * 512:(half + 1) * 512]:
                     e.tensor_tensor(p, p, bb, ALU.add),
                     rd=[f"pb{wb[half]}", "bobc"], wr=[f"pb{wb[half]}"])
                xv = x_sb.ap()[:, tau * D + half * 512:
                               tau * D + half * 512 + 512]
                E.op("dve", lambda e, xv=xv, p=pb[wb[half]].ap():
                     e.scalar_tensor_tensor(xv, p, 1.0, xv,
                                            ALU.mult, ALU.add),
                     rd=[f"pb{wb[half]}", f"x.{tau}"], wr=[f"x.{tau}"])
        return pro, wo_tau

    if DO_SUB1:
        ffn(first=True)
    wo_units = None
    if DO_ATTN:
        E.fence()
        attention()
        wo_units = wo_phase_units()
        if not DO_SUB3:
            pro, wo_tau = wo_units
            pro()
            for tau in range(NTAU):
                wo_tau(tau)
    if DO_SUB3:
        ffn(first=False, wo=wo_units)

    o_src = x_sb.ap().rearrange("p (t d) -> p t d", d=D)
    o_dst = out_d.rearrange("(t p) d -> p t d", p=128)
    for tau in range(NTAU):
        dma(o_dst[:, tau:tau + 1, :], o_src[:, tau:tau + 1, :],
            [f"x.{tau}"], ["out"], tag="store")
    E.op("sp", lambda e: e.nop(), rd=["out"], tag="final")

    E.emit()
    return nc


def host_layout(L0_attn_x, L0_attn_Wq, L0_attn_Wk, L0_attn_Wv, L0_attn_Wo,
                L0_attn_bo, L0_attn_proj, L0_ffn_w1, L0_ffn_b1, L0_ffn_w2,
                L0_ffn_b2):
    import ml_dtypes
    x = np.ascontiguousarray(np.asarray(L0_attn_x, dtype=np.float32))
    w1 = np.asarray(L0_ffn_w1, dtype=np.float32)
    w2 = np.asarray(L0_ffn_w2, dtype=np.float32)
    wq = np.asarray(L0_attn_Wq, dtype=np.float32)
    wk = np.asarray(L0_attn_Wk, dtype=np.float32)
    wv = np.asarray(L0_attn_Wv, dtype=np.float32)
    wo = np.asarray(L0_attn_Wo, dtype=np.float32)
    proj = np.asarray(L0_attn_proj, dtype=np.float32)
    b1 = np.asarray(L0_ffn_b1, dtype=np.float32)
    b2 = np.asarray(L0_ffn_b2, dtype=np.float32)
    bo = np.asarray(L0_attn_bo, dtype=np.float32)

    w1s = np.ascontiguousarray(w1.T.reshape(D, HC, 128).transpose(1, 0, 2))
    w2t = np.ascontiguousarray(w2.T)
    wqs = np.ascontiguousarray(wq.T.reshape(D, DC, 128).transpose(1, 0, 2))
    wks = np.ascontiguousarray(wk.T.reshape(D, DC, 128).transpose(1, 0, 2))
    wvt = np.ascontiguousarray(wv.T)
    wot = np.ascontiguousarray(wo.T)
    projp = np.zeros((NH, 128, M), np.float32)
    for h in range(NH):
        r0 = 64 * (h % 2)
        projp[h, r0:r0 + 64, :] = proj[h].T
    b1t = np.ascontiguousarray(b1.reshape(HC, 128).T)
    idb = np.eye(128).astype(ml_dtypes.bfloat16)
    idf = np.eye(2).astype(np.float32)
    onesb = np.ones(128).astype(ml_dtypes.bfloat16)
    onesf = np.ones(128).astype(np.float32)
    blk2 = np.zeros((128, 2), np.float32)
    blk2[0:64, 0] = 1
    blk2[64:128, 1] = 1
    blk2 = blk2.astype(ml_dtypes.bfloat16)
    shared = {"w1s": w1s, "w2t": w2t, "wqs": wqs, "wks": wks, "wvt": wvt,
              "wot": wot, "projp": projp, "b1t": b1t, "b2": b2, "bo": bo,
              "idb": idb, "idf": idf, "onesb": onesb, "onesf": onesf, "blk2": blk2}
    return [dict(shared, x=np.ascontiguousarray(x[b])) for b in range(8)]


def kernel(**inputs):
    in_maps = host_layout(**inputs)
    nc = build_core_program()
    res = run_bass_kernel_spmd(nc, in_maps, core_ids=list(range(8)))
    out = np.stack([np.asarray(res.results[b]["out"]) for b in range(8)],
                   axis=0)
    return out.astype(np.float32)





